# version 01 (final)
# baseline (speedup 1.0000x reference)
import contextlib
import numpy as np
import concourse.bass as bass
import concourse.mybir as mybir

F32 = mybir.dt.float32
BF16 = mybir.dt.bfloat16
AF = mybir.ActivationFunctionType
OP = mybir.AluOpType
AX = mybir.AxisListType

T = 4096
D = 1024
TC = 512
NTC = T // TC
NF = D // 128
NDC = 16
NST = 16


class Ctx:
    def __init__(self, nc, st):
        self.nc = nc
        self.st = st
        self.sem = {}
        self.cnt = {}
        self.t = {}

    def semaphore(self, name):
        if name not in self.sem:
            self.sem[name] = self.st.enter_context(self.nc.semaphore(name))
            self.cnt[name] = 0
        return self.sem[name]

    def sbuf(self, name, shape, dtype):
        h = self.st.enter_context(self.nc.sbuf_tensor(name, shape, dtype))
        self.t[name] = h
        return h

    def psum(self, name, shape, dtype=F32):
        h = self.st.enter_context(self.nc.psum_tensor(name, shape, dtype))
        self.t[name] = h
        return h

    def bank(self, i, w=1):
        return self.t["ps"][:, i * 512:(i + w) * 512]


def emit_ln(ctx, x_tiles, out_tiles, tc, double=False, gamma_cols=None,
            beta_cols=None, plain_out_tiles=None, pre_wait=None, eps=1e-5):
    nc = ctx.nc
    nfeat = 128 * len(x_tiles)
    sq = ctx.t["ln_sq"]
    rows = ctx.t["ln_rows"]
    m_r = rows[0:1, 0 * tc:1 * tc]
    q_r = rows[0:1, 1 * tc:2 * tc]
    v_r = rows[0:1, 2 * tc:3 * tc]
    r1_r = rows[0:1, 3 * tc:4 * tc]
    c1_r = rows[0:1, 4 * tc:5 * tc]
    c0_r = rows[0:1, 5 * tc:6 * tc]
    c0a_r = rows[0:1, 6 * tc:7 * tc]
    ones_col = ctx.t["ones_col"]
    ones_row = ctx.t["ones_row"]
    sums = ctx.bank(4)[0:1, 0:tc]
    sumsq = ctx.bank(5)[0:1, 0:tc]
    c0B = ctx.bank(6)[:, 0:tc]
    c1B = ctx.bank(7)[:, 0:tc]
    s_sq = ctx.semaphore("ln_sq_s")
    s_st = ctx.semaphore("ln_st")
    s_v1 = ctx.semaphore("ln_v1")
    s_a1 = ctx.semaphore("ln_a1")
    s_v2 = ctx.semaphore("ln_v2")
    s_a2 = ctx.semaphore("ln_a2")
    s_c = ctx.semaphore("ln_c")
    s_c2 = ctx.semaphore("ln_c2")
    b_sq, b_st, b_v1, b_a1 = (ctx.cnt["ln_sq_s"], ctx.cnt["ln_st"],
                              ctx.cnt["ln_v1"], ctx.cnt["ln_a1"])
    b_v2, b_a2, b_c = ctx.cnt["ln_v2"], ctx.cnt["ln_a2"], ctx.cnt["ln_c"]
    b_c2 = ctx.cnt["ln_c2"]

    x_is_bf = (x_tiles[0].dtype == BF16)
    sums_ones = ctx.t["ones_col_bf"] if x_is_bf else ones_col
    with nc.Block() as b:
        @b.scalar
        def _(s):
            if pre_wait:
                s.wait_ge(ctx.sem[pre_wait], ctx.cnt[pre_wait])
            for i, xt in enumerate(x_tiles):
                ins = s.activation(sq[:, i * tc:(i + 1) * tc], xt, AF.Square)
            ins.then_inc(s_sq, 1)
            s.wait_ge(s_v1, b_v1 + 1)
            s.activation(r1_r, v_r, AF.Sqrt, bias=eps).then_inc(s_a1, 1)
            if double:
                s.wait_ge(s_v2, b_v2 + 1)
                s.activation(q_r, q_r, AF.Sqrt, bias=eps).then_inc(s_a2, 1)

        @b.tensor
        def _(t):
            t.wait_ge(s_sq, b_sq + 1)
            for i, xt in enumerate(x_tiles):
                t.matmul(sums, sums_ones[:], xt,
                         start=(i == 0), stop=(i == len(x_tiles) - 1))
            for i in range(len(x_tiles)):
                ins = t.matmul(sumsq, ctx.t["ones_col_bf"][:],
                               sq[:, i * tc:(i + 1) * tc],
                               start=(i == 0), stop=(i == len(x_tiles) - 1))
            ins.then_inc(s_st, 1)
            if plain_out_tiles is not None:
                t.wait_ge(s_c2, b_c2 + 1)
                t.matmul(ctx.bank(4)[:, 0:tc], ones_row[:], r1_r,
                         start=True, stop=True)
                t.matmul(ctx.bank(5)[:, 0:tc], ones_row[:], c0a_r,
                         start=True, stop=True)
            t.wait_ge(s_c, b_c + 1)
            t.matmul(c0B, ones_row[:], c0_r, start=True, stop=True)
            t.matmul(c1B, ones_row[:], c1_r, start=True, stop=True)

        @b.vector
        def _(v):
            v.wait_ge(s_st, b_st + 1)
            v.tensor_scalar_mul(m_r, sums, 1.0 / nfeat)
            v.tensor_tensor(q_r, m_r, m_r, OP.mult)
            v.scalar_tensor_tensor(v_r, sumsq, 1.0 / nfeat, q_r,
                                   OP.mult, OP.subtract).then_inc(s_v1, 1)
            v.wait_ge(s_a1, b_a1 + 1)
            v.reciprocal(r1_r, r1_r)
            if plain_out_tiles is not None:
                v.tensor_tensor(c0a_r, m_r, r1_r, OP.mult)
                v.tensor_scalar_mul(c0a_r, c0a_r, -1.0).then_inc(s_c2, 1)
            if double:
                v.tensor_tensor(q_r, r1_r, r1_r, OP.mult)
                v.tensor_tensor(q_r, q_r, v_r, OP.mult).then_inc(s_v2, 1)
                v.wait_ge(s_a2, b_a2 + 1)
                v.reciprocal(q_r, q_r)
                v.tensor_tensor(c1_r, r1_r, q_r, OP.mult)
            else:
                v.tensor_copy(c1_r, r1_r)
            v.tensor_tensor(c0_r, m_r, c1_r, OP.mult)
            v.tensor_scalar_mul(c0_r, c0_r, -1.0).then_inc(s_c, 1)
    for s in ("ln_sq_s", "ln_st", "ln_v1", "ln_a1", "ln_c"):
        ctx.cnt[s] += 1
    if double:
        ctx.cnt["ln_v2"] += 1
        ctx.cnt["ln_a2"] += 1
    if plain_out_tiles is not None:
        ctx.cnt["ln_c2"] += 1

    sp = ctx.t["ln_apply_scr"]
    with nc.Block() as b:
        @b.vector
        def _(v):
            for i, xt in enumerate(x_tiles):
                v.tensor_tensor(sp[:, 0:tc], xt, c1B, OP.mult)
                if gamma_cols is None:
                    v.tensor_tensor(out_tiles[i], sp[:, 0:tc], c0B, OP.add)
                else:
                    v.tensor_tensor(sp[:, 0:tc], sp[:, 0:tc], c0B, OP.add)
                    v.tensor_scalar(out_tiles[i], sp[:, 0:tc], gamma_cols[i],
                                    beta_cols[i], OP.mult, OP.add)
                if plain_out_tiles is not None:
                    v.tensor_tensor(sp[:, 0:tc], xt, ctx.bank(4)[:, 0:tc],
                                    OP.mult)
                    v.tensor_tensor(plain_out_tiles[i], sp[:, 0:tc],
                                    ctx.bank(5)[:, 0:tc], OP.add)


def emit_mamba(ctx, xin, xout, W):
    nc = ctx.nc
    ps = ctx.t["ps"]

    w_in = ctx.sbuf("mb_w_in", [128, 8 * 4096], BF16)
    wo_buf = ctx.sbuf("mb_wo_buf", [128, 2 * 2048], BF16)
    w_x = ctx.sbuf("mb_w_x", [128, 16 * 96], BF16)
    w_dt = ctx.sbuf("mb_w_dt", [64, 2048], BF16)
    conv_w = ctx.sbuf("mb_conv_w", [128, 64], F32)
    conv_b = ctx.sbuf("mb_conv_b", [128, 16], F32)
    b_dt = ctx.sbuf("mb_b_dt", [128, 16], F32)
    d_par = ctx.sbuf("mb_d", [128, 16], F32)
    a_sb = ctx.sbuf("mb_a", [128, 256], F32)
    lnw = ctx.sbuf("mb_lnw", [128, 8], F32)
    lnb = ctx.sbuf("mb_lnb", [128, 8], F32)
    nh16 = ctx.sbuf("mb_nh16", [16, 16 * 128], BF16)

    xc = ctx.sbuf("mb_xc", [128, 4096], F32)
    xn = ctx.sbuf("mb_xn", [128, 4096], BF16)
    z1 = ctx.sbuf("mb_z1", [128, 4096], BF16)
    xi = ctx.sbuf("mb_xi", [128, 16 * 516], BF16)
    z = ctx.sbuf("mb_z", [128, 16 * 512], BF16)
    u = ctx.sbuf("mb_u", [128, 16 * 512], BF16)
    dtl = ctx.sbuf("mb_dtl", [64, 512], BF16)
    bssm = ctx.sbuf("mb_bssm", [16, 512], BF16)
    cssm = ctx.sbuf("mb_cssm", [16, 512], BF16)
    dt = ctx.sbuf("mb_dt", [128, 512], F32)
    dtu = ctx.sbuf("mb_dtu", [128, 512], BF16)
    da = ctx.sbuf("mb_da", [128, 2 * 512], F32)
    db = ctx.sbuf("mb_db", [128, 512], F32)
    hh = ctx.sbuf("mb_h", [128, 512], F32)
    yy = ctx.sbuf("mb_y", [128, 512], F32)
    y2 = hh
    szt = db
    carry = ctx.sbuf("mb_carry", [128, 256], F32)
    sp1 = da[0:128, 0:512]
    sp2 = da[0:128, 512:1024]

    s_ld = ctx.semaphore("mb_ld")
    pairs = [(conv_w[:], W["conv_w_r"]), (conv_b[:], W["conv_b_r"]),
             (b_dt[:], W["b_dt_r"]), (d_par[:], W["d_r"]),
             (a_sb[:], W["log_a_r"]), (lnw[:], W["ln_w_r"]), (lnb[:], W["ln_b_r"])]
    casts = [(w_in.ap().rearrange("p (kc i) -> p kc i", kc=8),
              W["w_inT"].rearrange("(kc p) i -> p kc i", p=128)),
             (w_x.ap().rearrange("p (kc i) -> p kc i", kc=16),
              W["w_xT"].rearrange("(kc p) i -> p kc i", p=128)),
             (w_dt[:], W["w_dtT"]), (nh16[:], W["nh16"])]
    with nc.Block() as b:
        @b.sync
        def _(sy):
            for dst, src in pairs:
                sy.dma_start(dst, src).then_inc(s_ld, 16)
        @b.gpsimd
        def _(g):
            for dst, src in casts:
                g.dma_start(dst, src).then_inc(s_ld, 16)
    ctx.cnt["mb_ld"] += 16 * (len(pairs) + len(casts))
    with nc.Block() as b:
        @b.scalar
        def _(s):
            s.wait_ge(s_ld, ctx.cnt["mb_ld"])
            s.activation(a_sb[:], a_sb[:], AF.Exp)
        @b.vector
        def _(v):
            v.wait_ge(s_ld, ctx.cnt["mb_ld"])
            v.memset(carry[:], 0.0)
            for dc in range(NDC):
                v.memset(xi[:, dc * 516:dc * 516 + 4], 0.0)
    with nc.Block() as b:
        @b.vector
        def _(v):
            v.tensor_scalar_mul(a_sb[:], a_sb[:], -1.0)

    xin_r = xin.rearrange("(fc p) t -> p fc t", p=128)
    xout_r = xout.rearrange("(fc p) t -> p fc t", p=128)
    xc_r = xc.ap().rearrange("p (fc t) -> p fc t", t=TC)
    s_st = ctx.semaphore("mb_st")
    for c in range(NTC):
        t0 = c * TC
        with nc.Block() as b:
            @b.sync
            def _(sy):
                sy.wait_ge(s_st, ctx.cnt["mb_st"])
                sy.dma_start(xc_r, xin_r[:, :, t0:t0 + TC]).then_inc(s_ld, 16)
        ctx.cnt["mb_ld"] += 16
        x_tiles = [xc[:, fc * TC:(fc + 1) * TC] for fc in range(NF)]
        xn_tiles = [xn[:, fc * TC:(fc + 1) * TC] for fc in range(NF)]
        emit_ln(ctx, x_tiles, xn_tiles, TC, double=True,
                gamma_cols=[lnw[:, fc:fc + 1] for fc in range(NF)],
                beta_cols=[lnb[:, fc:fc + 1] for fc in range(NF)],
                plain_out_tiles=[z1[:, fc * TC:(fc + 1) * TC] for fc in range(NF)],
                pre_wait="mb_ld")
        with nc.Block() as b:
            @b.vector
            def _(v):
                if c > 0:
                    for dc in range(NDC):
                        v.tensor_copy(xi[:, dc * 516 + 1:dc * 516 + 4],
                                      xi[:, dc * 516 + 513:dc * 516 + 516])
                else:
                    v.engine_nop()
        s_mm = ctx.semaphore("mb_mm")
        s_ev = ctx.semaphore("mb_ev")
        mm0, ev0 = ctx.cnt["mb_mm"], ctx.cnt["mb_ev"]
        with nc.Block() as b:
            @b.tensor
            def _(t):
                for ic in range(32):
                    if ic >= 4:
                        t.wait_ge(s_ev, ev0 + ic - 3)
                    for kc in range(NF):
                        ins = t.matmul(
                            ps[:, (ic % 4) * 512:(ic % 4) * 512 + TC],
                            w_in[:, kc * 4096 + ic * 128:kc * 4096 + (ic + 1) * 128],
                            xn[:, kc * TC:(kc + 1) * TC],
                            start=(kc == 0), stop=(kc == NF - 1))
                    ins.then_inc(s_mm, 1)
            @b.scalar
            def _(s):
                for ic in range(32):
                    s.wait_ge(s_mm, mm0 + ic + 1)
                    src = ps[:, (ic % 4) * 512:(ic % 4) * 512 + TC]
                    if ic < 16:
                        dst = xi[:, ic * 516 + 4:ic * 516 + 516]
                    else:
                        dst = z[:, (ic - 16) * TC:(ic - 15) * TC]
                    s.copy(dst, src).then_inc(s_ev, 1)
        ctx.cnt["mb_mm"] += 32
        ctx.cnt["mb_ev"] += 32
        s_cv = ctx.semaphore("mb_cv")
        s_si = ctx.semaphore("mb_si")
        cv0, si0 = ctx.cnt["mb_cv"], ctx.cnt["mb_si"]
        with nc.Block() as b:
            @b.vector
            def _(v):
                for dc in range(NDC):
                    scr = sp1 if dc % 2 == 0 else sp2
                    if dc >= 2:
                        v.wait_ge(s_si, si0 + dc - 1)
                    base = dc * 516
                    v.tensor_scalar(scr, xi[:, base + 1:base + 513],
                                    conv_w[:, dc * 4:dc * 4 + 1], None, OP.mult)
                    for k in range(1, 4):
                        ins = v.scalar_tensor_tensor(
                            scr, xi[:, base + 1 + k:base + 513 + k],
                            conv_w[:, dc * 4 + k:dc * 4 + k + 1],
                            scr, OP.mult, OP.add)
                    ins.then_inc(s_cv, 1)
            @b.scalar
            def _(s):
                for dc in range(NDC):
                    scr = sp1 if dc % 2 == 0 else sp2
                    s.wait_ge(s_cv, cv0 + dc + 1)
                    s.activation(u[:, dc * TC:(dc + 1) * TC], scr, AF.Silu,
                                 bias=conv_b[:, dc:dc + 1]).then_inc(s_si, 1)
        ctx.cnt["mb_cv"] += NDC
        ctx.cnt["mb_si"] += NDC
        s_m5 = ctx.semaphore("mb_m5")
        m50 = ctx.cnt["mb_m5"]
        with nc.Block() as b:
            @b.tensor
            def _(t):
                for kc in range(NDC):
                    rhs = u[:, kc * TC:(kc + 1) * TC]
                    st_, sp_ = (kc == 0), (kc == NDC - 1)
                    t.matmul(ctx.bank(4)[0:64, :], w_x[:, kc * 96:kc * 96 + 64],
                             rhs, start=st_, stop=sp_)
                    t.matmul(ctx.bank(5)[0:16, :], w_x[:, kc * 96 + 64:kc * 96 + 80],
                             rhs, start=st_, stop=sp_)
                    ins = t.matmul(ctx.bank(6)[0:16, :],
                                   w_x[:, kc * 96 + 80:kc * 96 + 96],
                                   rhs, start=st_, stop=sp_)
                ins.then_inc(s_m5, 1)
            @b.scalar
            def _(s):
                s.wait_ge(s_m5, m50 + 1)
                s.copy(dtl[:], ctx.bank(4)[0:64, :])
                s.copy(bssm[:], ctx.bank(5)[0:16, :])
                s.copy(cssm[:], ctx.bank(6)[0:16, :])
        ctx.cnt["mb_m5"] += 1
        for dc in range(NDC):
            s_wdt = ctx.semaphore("mb_wdt")
            s_sp = ctx.semaphore("mb_sp")
            s_dtc = ctx.semaphore("mb_dtc")
            s_rep = ctx.semaphore("mb_rep")
            s_da = ctx.semaphore("mb_da")
            s_scan = ctx.semaphore("mb_scan")
            s_cons = ctx.semaphore("mb_cons")
            s_sz = ctx.semaphore("mb_szs")
            wdt0, sp0, dtc0 = (ctx.cnt["mb_wdt"], ctx.cnt["mb_sp"],
                               ctx.cnt["mb_dtc"])
            rep0, da0, scan0, cons0 = (ctx.cnt["mb_rep"], ctx.cnt["mb_da"],
                                       ctx.cnt["mb_scan"], ctx.cnt["mb_cons"])
            sz0 = ctx.cnt["mb_szs"]
            bB = [ctx.bank(5), ctx.bank(6)]
            bC = [ctx.bank(7), ctx.bank(4)]
            with nc.Block() as b:
                @b.tensor
                def _(t):
                    t.matmul(ctx.bank(4), w_dt[:, dc * 128:(dc + 1) * 128], dtl[:],
                             start=True, stop=True).then_inc(s_wdt, 1)
                    for n in range(NST):
                        if n == 1:
                            t.wait_ge(s_sp, sp0 + 1)
                        if n >= 2:
                            t.wait_ge(s_cons, cons0 + n - 1)
                        t.matmul(bB[n % 2], nh16[:, n * 128:(n + 1) * 128], bssm[:],
                                 start=True, stop=True)
                        t.matmul(bC[n % 2], nh16[:, n * 128:(n + 1) * 128], cssm[:],
                                 start=True, stop=True).then_inc(s_rep, 1)
                @b.scalar
                def _(s):
                    s.wait_ge(s_wdt, wdt0 + 1)
                    s.activation(sp1, ctx.bank(4), AF.Abs, bias=b_dt[:, dc:dc + 1])
                    s.activation(sp2, ctx.bank(4), AF.Relu, bias=b_dt[:, dc:dc + 1])
                    s.activation(sp1, sp1, AF.Exp, scale=-1.0)
                    s.activation(sp1, sp1, AF.Ln, bias=1.0).then_inc(s_sp, 1)
                    s.wait_ge(s_dtc, dtc0 + 1)
                    for n in range(NST):
                        if n >= 2:
                            s.wait_ge(s_scan, scan0 + n - 1)
                        s.activation(da[:, (n % 2) * 512:(n % 2) * 512 + TC],
                                     dt[:], AF.Exp,
                                     scale=a_sb[:, dc * 16 + n:dc * 16 + n + 1]
                                     ).then_inc(s_da, 1)
                    s.activation(szt[:], z[:, dc * TC:(dc + 1) * TC],
                                 AF.Silu).then_inc(s_sz, 1)
                @b.vector
                def _(v):
                    v.wait_ge(s_sp, sp0 + 1)
                    v.tensor_tensor(sp1, sp1, sp2, OP.add)
                    v.tensor_scalar(dt[:], sp1, 1e-3, 0.1, OP.add, OP.min)
                    v.tensor_tensor(dtu[:], dt[:],
                                    u[:, dc * TC:(dc + 1) * TC],
                                    OP.mult).then_inc(s_dtc, 1)
                    for n in range(NST):
                        v.wait_ge(s_rep, rep0 + n + 1)
                        v.tensor_tensor(db[:], dtu[:], bB[n % 2], OP.mult)
                        v.wait_ge(s_da, da0 + n + 1)
                        v.tensor_tensor_scan(
                            hh[:], da[:, (n % 2) * 512:(n % 2) * 512 + TC], db[:],
                            carry[:, dc * 16 + n:dc * 16 + n + 1],
                            OP.mult, OP.add).then_inc(s_scan, 1)
                        v.tensor_copy(carry[:, dc * 16 + n:dc * 16 + n + 1],
                                      hh[:, TC - 1:TC])
                        if n == 0:
                            v.tensor_tensor(yy[:], hh[:], bC[0],
                                            OP.mult).then_inc(s_cons, 1)
                        else:
                            v.tensor_tensor(hh[:], hh[:], bC[n % 2], OP.mult)
                            v.tensor_tensor(yy[:], yy[:], hh[:],
                                            OP.add).then_inc(s_cons, 1)
                    v.scalar_tensor_tensor(y2[:], u[:, dc * TC:(dc + 1) * TC],
                                           d_par[:, dc:dc + 1], yy[:],
                                           OP.mult, OP.add)
                    v.wait_ge(s_sz, sz0 + 1)
                    v.tensor_tensor(z[:, dc * TC:(dc + 1) * TC], y2[:], szt[:],
                                    OP.mult)
            ctx.cnt["mb_wdt"] += 1
            ctx.cnt["mb_sp"] += 1
            ctx.cnt["mb_dtc"] += 1
            ctx.cnt["mb_rep"] += NST
            ctx.cnt["mb_da"] += NST
            ctx.cnt["mb_scan"] += NST
            ctx.cnt["mb_cons"] += NST
            ctx.cnt["mb_szs"] += 1
        s_m9 = ctx.semaphore("mb_m9")
        s_e9 = ctx.semaphore("mb_e9")
        s_wo = ctx.semaphore("mb_wo")
        m90, e90, wo0 = ctx.cnt["mb_m9"], ctx.cnt["mb_e9"], ctx.cnt["mb_wo"]
        woT_r = W["w_outT"].rearrange("(kc p) i -> p kc i", p=128)
        wo_r = wo_buf.ap().rearrange("p (h kc i) -> p h kc i", h=2, i=128)
        with nc.Block() as b:
            @b.gpsimd
            def _(g):
                for ic in range(NF):
                    if ic >= 2:
                        g.wait_ge(s_m9, m90 + ic - 1)
                    g.dma_start(wo_r[:, ic % 2],
                                woT_r[:, :, ic * 128:(ic + 1) * 128]
                                ).then_inc(s_wo, 16)
            @b.tensor
            def _(t):
                for ic in range(NF):
                    if ic >= 4:
                        t.wait_ge(s_e9, e90 + ic - 3)
                    t.wait_ge(s_wo, wo0 + 16 * (ic + 1))
                    for kc in range(NDC):
                        ins = t.matmul(
                            ps[:, (ic % 4) * 512:(ic % 4) * 512 + TC],
                            wo_buf[:, (ic % 2) * 2048 + kc * 128:
                                   (ic % 2) * 2048 + (kc + 1) * 128],
                            z[:, kc * TC:(kc + 1) * TC],
                            start=(kc == 0), stop=(kc == NDC - 1))
                    ins.then_inc(s_m9, 1)
            @b.vector
            def _(v):
                for ic in range(NF):
                    v.wait_ge(s_m9, m90 + ic + 1)
                    v.tensor_tensor(xc[:, ic * TC:(ic + 1) * TC],
                                    ps[:, (ic % 4) * 512:(ic % 4) * 512 + TC],
                                    xc[:, ic * TC:(ic + 1) * TC],
                                    OP.add).then_inc(s_e9, 1)
                    v.tensor_tensor(xc[:, ic * TC:(ic + 1) * TC],
                                    xc[:, ic * TC:(ic + 1) * TC],
                                    z1[:, ic * TC:(ic + 1) * TC], OP.add)
        ctx.cnt["mb_m9"] += NF
        ctx.cnt["mb_e9"] += NF
        ctx.cnt["mb_wo"] += 16 * NF
        with nc.Block() as b:
            @b.sync
            def _(sy):
                sy.dma_start(xout_r[:, :, t0:t0 + TC], xc_r).then_inc(s_st, 16)
        ctx.cnt["mb_st"] += 16


def finish(ctx, sems):
    nc = ctx.nc
    with nc.Block() as b:
        @b.sync
        def _(sy):
            for s in sems:
                sy.wait_ge(ctx.sem[s], ctx.cnt[s])


def alloc_common(ctx, tc=512):
    nc = ctx.nc
    ctx.psum("ps", [128, 8 * 512])
    ctx.sbuf("ones_col", [128, 1], F32)
    ctx.sbuf("ones_col_bf", [128, 1], BF16)
    ctx.sbuf("ones_row", [1, 128], F32)
    ctx.sbuf("ln_sq", [128, 8 * tc], BF16)
    ctx.sbuf("ln_rows", [1, 7 * tc], F32)
    ctx.sbuf("ln_apply_scr", [128, tc], F32)
    cst = ctx.sbuf("consts", [128, 4], F32)
    nc.const_aps.register(F32, 1e-05, cst[:, 0:1])
    nc.const_aps.register(F32, 1.0, cst[:, 1:2])
    nc.const_aps.register(F32, 1e-20, cst[:, 2:3])
    nc.const_aps.register(F32, 0.0, cst[:, 3:4])
    with nc.Block() as b:
        @b.vector
        def _(v):
            v.memset(ctx.t["ones_col"][:], 1.0)
            v.memset(ctx.t["ones_col_bf"][:], 1.0)
            v.memset(ctx.t["ones_row"][:], 1.0)
            v.memset(cst[:, 0:1], 1e-05)
            v.memset(cst[:, 1:2], 1.0)
            v.memset(cst[:, 2:3], 1e-20)
            v.memset(cst[:, 3:4], 0.0)


def prep_mamba_weights(inp):
    f = np.ascontiguousarray
    W = {}
    W["w_inT"] = f(inp["L0_attn_W_in"].T)
    W["w_outT"] = f(inp["L0_attn_W_out"].T)
    W["w_xT"] = f(inp["L0_attn_W_x"].T)
    W["w_dtT"] = f(inp["L0_attn_W_dt"].T)
    W["conv_w_r"] = f(inp["L0_attn_conv_w"].reshape(16, 128, 4)
                      .transpose(1, 0, 2).reshape(128, 64))
    W["conv_b_r"] = f(inp["L0_attn_conv_b"].reshape(16, 128).T)
    W["b_dt_r"] = f(inp["L0_attn_b_dt"].reshape(16, 128).T)
    W["d_r"] = f(inp["L0_attn_D_param"].reshape(16, 128).T)
    W["log_a_r"] = f(inp["L0_attn_log_A"].reshape(16, 128, 16)
                     .transpose(1, 0, 2).reshape(128, 256))
    W["ln_w_r"] = f(inp["L0_attn_ln_w"].reshape(8, 128).T)
    W["ln_b_r"] = f(inp["L0_attn_ln_b"].reshape(8, 128).T)
    nh = np.zeros((16, 16 * 128), np.float32)
    for n in range(16):
        nh[n, n * 128:(n + 1) * 128] = 1.0
    W["nh16"] = nh
    return W


MB_SHAPES = dict(
    w_inT=(1024, 4096), w_outT=(2048, 1024), w_xT=(2048, 96), w_dtT=(64, 2048),
    conv_w_r=(128, 64), conv_b_r=(128, 16), b_dt_r=(128, 16), d_r=(128, 16),
    log_a_r=(128, 256), ln_w_r=(128, 8), ln_b_r=(128, 8), nh16=(16, 2048))


def build_mamba_program():
    nc = bass.Bass("TRN2", target_bir_lowering=False)
    W = {k: nc.dram_tensor("i_" + k, list(s), F32, kind="ExternalInput").ap()
         for k, s in MB_SHAPES.items()}
    xin = nc.dram_tensor("x", [1024, 4096], F32, kind="ExternalInput").ap()
    xout = nc.dram_tensor("xo", [1024, 4096], F32, kind="ExternalOutput").ap()
    st = contextlib.ExitStack()
    with st:
        ctx = Ctx(nc, st)
        alloc_common(ctx)
        emit_mamba(ctx, xin, xout, W)
        finish(ctx, ["mb_st"])
    return nc


def emit_proj(ctx, tag, w_sb, xn, out_sb, n_ic, n_kc, tc=TC, out_bf=True,
              w_col=1024, xsrc=None, rot=4, wait_store=None, ic_off=0):
    nc = ctx.nc
    ps = ctx.t["ps"]
    s_mm = ctx.semaphore(tag + "_mm")
    s_ev = ctx.semaphore(tag + "_ev")
    mm0, ev0 = ctx.cnt[tag + "_mm"], ctx.cnt[tag + "_ev"]
    xsrc = xsrc if xsrc is not None else (lambda kc: xn[:, kc * tc:(kc + 1) * tc])
    with nc.Block() as b:
        @b.tensor
        def _(t):
            for ic in range(n_ic):
                if ic >= rot:
                    t.wait_ge(s_ev, ev0 + ic - rot + 1)
                for kc in range(n_kc):
                    ins = t.matmul(
                        ps[:, (ic % rot) * 512:(ic % rot) * 512 + tc],
                        w_sb[:, kc * w_col + (ic + ic_off) * 128:
                             kc * w_col + (ic + ic_off + 1) * 128],
                        xsrc(kc), start=(kc == 0), stop=(kc == n_kc - 1))
                ins.then_inc(s_mm, 1)
        @b.scalar
        def _(s):
            if wait_store is not None:
                s.wait_ge(ctx.sem[wait_store], ctx.cnt[wait_store])
            for ic in range(n_ic):
                s.wait_ge(s_mm, mm0 + ic + 1)
                s.copy(out_sb[:, ic * tc:(ic + 1) * tc],
                       ps[:, (ic % rot) * 512:(ic % rot) * 512 + tc]
                       ).then_inc(s_ev, 1)
    ctx.cnt[tag + "_mm"] += n_ic
    ctx.cnt[tag + "_ev"] += n_ic


def emit_l1attn(ctx, xin, xout, W, stages=4):
    nc = ctx.nc
    ps = ctx.t["ps"]
    wq = ctx.sbuf("la_wq", [128, 8 * 1024], BF16)
    wk = ctx.sbuf("la_wk", [128, 8 * 1024], BF16)
    wv = ctx.sbuf("la_wv", [128, 8 * 1024], BF16)
    wo = ctx.sbuf("la_wo", [64, 16 * 1024], BF16)
    bo = ctx.sbuf("la_bo", [128, 8], F32)
    ident = ctx.sbuf("la_id", [128, 128], BF16)
    maskI = ctx.sbuf("la_mask", [128, 1024], BF16)
    rep16 = ctx.sbuf("la_rep16", [16, 1024], F32)
    hcsel = ctx.sbuf("la_hcsel", [16, 8 * 128], BF16)
    selA = ctx.sbuf("la_selA", [128, 8 * 16], F32)
    xc = ctx.sbuf("la_xc", [128, 4096], F32)
    xn = ctx.sbuf("la_xn", [128, 8 * TC], BF16)
    qf = ctx.sbuf("la_qf", [128, 8 * TC], BF16)
    kf = ctx.sbuf("la_kf", [128, 8 * TC], BF16)
    vf = ctx.sbuf("la_vf", [128, 8 * TC], BF16)
    qtk = ctx.sbuf("la_qtk", [128, 2 * 512], BF16)
    ktk = ctx.sbuf("la_ktk", [128, 2 * 1024], BF16)
    gsc = xn.ap().bitcast(F32)[:, 0:1024]
    hred = ctx.sbuf("la_hred", [128, 2 * 16], F32)
    att = ctx.sbuf("la_att", [16, 16], F32)
    arow = ctx.sbuf("la_arow", [16, 3], F32)
    atr = ctx.sbuf("la_atr", [16, 1024], BF16)
    bas = ctx.sbuf("la_bas", [128, 8 * 1024], BF16)
    oa = ctx.sbuf("la_oa", [128, 4096], BF16)
    tstage = oa
    xo_sb = ctx.t["ln_sq"].ap().bitcast(F32)

    qtok_h = nc.dram_tensor("la_qtok", [4096, 1024], BF16, kind="Internal").ap()
    ktok_h = nc.dram_tensor("la_ktok", [4096, 1024], BF16, kind="Internal").ap()
    vfe_h = nc.dram_tensor("la_vfe", [1024, 4096], BF16, kind="Internal").ap()
    ofe_h = nc.dram_tensor("la_ofe", [1024, 4096], BF16, kind="Internal").ap()

    s_ld = ctx.semaphore("la_ld")
    pairs = [(bo[:], W["bo_r"]), (rep16[:], W["rep16"]), (selA[:], W["selA"])]
    casts = [(wq.ap().rearrange("p (kc i) -> p kc i", kc=8),
              W["wqT"].rearrange("(kc p) i -> p kc i", p=128)),
             (wk.ap().rearrange("p (kc i) -> p kc i", kc=8),
              W["wkT"].rearrange("(kc p) i -> p kc i", p=128)),
             (wv.ap().rearrange("p (kc i) -> p kc i", kc=8),
              W["wvT"].rearrange("(kc p) i -> p kc i", p=128)),
             (wo.ap().rearrange("p (sm i) -> p sm i", sm=16),
              W["woT"].rearrange("(sm p) i -> p sm i", p=64)),
             (ident[:], W["ident"]), (maskI[:], W["maskI"]),
             (hcsel[:], W["hcsel"])]
    with nc.Block() as b:
        @b.sync
        def _(sy):
            for dst, src in pairs:
                sy.dma_start(dst, src).then_inc(s_ld, 16)
        @b.gpsimd
        def _(g):
            for dst, src in casts:
                g.dma_start(dst, src).then_inc(s_ld, 16)
    ctx.cnt["la_ld"] += 16 * (len(pairs) + len(casts))

    xin_r = xin.rearrange("(fc p) t -> p fc t", p=128)
    xout_r = xout.rearrange("(fc p) t -> p fc t", p=128)
    xc_r = xc.ap().rearrange("p (fc t) -> p fc t", t=TC)
    s_st = ctx.semaphore("la_st")
    for c in range(NTC):
        t0 = c * TC
        with nc.Block() as b:
            @b.sync
            def _(sy):
                sy.wait_ge(s_st, ctx.cnt["la_st"])
                sy.dma_start(xc_r, xin_r[:, :, t0:t0 + TC]).then_inc(s_ld, 16)
        ctx.cnt["la_ld"] += 16
        x_tiles = [xc[:, fc * TC:(fc + 1) * TC] for fc in range(NF)]
        xn_tiles = [xn[:, fc * TC:(fc + 1) * TC] for fc in range(NF)]
        emit_ln(ctx, x_tiles, xn_tiles, TC, double=False, pre_wait="la_ld")
        emit_proj(ctx, "la_q", wq, xn, qf, NF, NF)
        emit_proj(ctx, "la_k", wk, xn, kf, NF, NF)
        emit_proj(ctx, "la_v", wv, xn, vf, NF, NF, wait_store="la_st")
        with nc.Block() as b:
            @b.sync
            def _(sy):
                sy.dma_start(vfe_h.rearrange("(fc p) t -> p fc t", p=128)
                             [:, :, t0:t0 + TC],
                             vf.ap().rearrange("p (fc t) -> p fc t", t=TC)
                             ).then_inc(s_st, 16)
        ctx.cnt["la_st"] += 16
        for which, src, dst_h in (("q", qf, qtok_h), ("k", kf, ktok_h)):
            s_tp = ctx.semaphore("la_tp")
            s_te = ctx.semaphore("la_te")
            tp0, te0 = ctx.cnt["la_tp"], ctx.cnt["la_te"]
            with nc.Block() as b:
                @b.tensor
                def _(t):
                    idx = 0
                    for ic in range(NF):
                        for tt in range(4):
                            if idx >= 4:
                                t.wait_ge(s_te, te0 + idx - 3)
                            bb = ctx.bank(idx % 4).bitcast(BF16)
                            t.transpose(bb[:, 0:128],
                                        src[:, ic * TC + tt * 128:
                                            ic * TC + (tt + 1) * 128],
                                        ident[:]).then_inc(s_tp, 1)
                            idx += 1
                @b.scalar
                def _(s):
                    s.wait_ge(s_st, ctx.cnt["la_st"])
                    idx = 0
                    for ic in range(NF):
                        for tt in range(4):
                            s.wait_ge(s_tp, tp0 + idx + 1)
                            bb = ctx.bank(idx % 4).bitcast(BF16)
                            s.copy(tstage[:, tt * 1024 + ic * 128:
                                           tt * 1024 + (ic + 1) * 128],
                                   bb[:, 0:128]).then_inc(s_te, 1)
                            idx += 1
            ctx.cnt["la_tp"] += 32
            ctx.cnt["la_te"] += 32
            with nc.Block() as b:
                @b.sync
                def _(sy):
                    sy.dma_start(
                        dst_h[t0:t0 + TC, :].rearrange("(tt p) f -> p tt f", p=128),
                        tstage.ap().rearrange("p (tt f) -> p tt f", tt=4)
                        ).then_inc(s_st, 16)
            ctx.cnt["la_st"] += 16
    groups = [(0, 1, 2), (3, 4, 5), (6, 7)]
    first_sel = True
    for gp in groups:
        s_gld = ctx.semaphore("la_gld")
        s_gmm = ctx.semaphore("la_gmm")
        gld0, gmm0 = ctx.cnt["la_gld"], ctx.cnt["la_gmm"]
        ncols = len(gp) * 128
        with nc.Block() as b:
            @b.sync
            def _(sy):
                sy.wait_ge(s_st, ctx.cnt["la_st"])
                for sc in range(32):
                    if sc >= 2:
                        sy.wait_ge(s_gmm, gmm0 + sc - 1)
                    sy.dma_start(qtk[:, (sc % 2) * 512:(sc % 2) * 512 + ncols],
                                 qtok_h[sc * 128:(sc + 1) * 128,
                                        gp[0] * 128:gp[0] * 128 + ncols]
                                 ).then_inc(s_gld, 16)
                    sy.dma_start(ktk[:, (sc % 2) * 1024:(sc % 2 + 1) * 1024],
                                 ktok_h[sc * 128:(sc + 1) * 128, :]
                                 ).then_inc(s_gld, 16)
            @b.tensor
            def _(t):
                for sc in range(32):
                    t.wait_ge(s_gld, gld0 + 32 * (sc + 1))
                    for gi in range(len(gp)):
                        for jh in range(2):
                            ins = t.matmul(
                                ctx.bank(2 * gi + jh),
                                qtk[:, (sc % 2) * 512 + gi * 128:
                                    (sc % 2) * 512 + (gi + 1) * 128],
                                ktk[:, (sc % 2) * 1024 + jh * 512:
                                    (sc % 2) * 1024 + (jh + 1) * 512],
                                start=(sc == 0), stop=(sc == 31))
                    ins.then_inc(s_gmm, 1)
        ctx.cnt["la_gld"] += 32 * 32
        ctx.cnt["la_gmm"] += 32
        s_hr = ctx.semaphore("la_hr")
        s_hc = ctx.semaphore("la_hc")
        hr0, hc0 = ctx.cnt["la_hr"], ctx.cnt["la_hc"]
        with nc.Block() as b:
            @b.vector
            def _(v):
                for gi, ic in enumerate(gp):
                    if gi >= 2:
                        v.wait_ge(s_hc, hc0 + gi - 1)
                    v.tensor_tensor(gsc, ctx.bank(2 * gi, 2), maskI[:], OP.mult)
                    v.reduce_sum(hred[:, (gi % 2) * 16:(gi % 2) * 16 + 16]
                                 .unsqueeze(2),
                                 gsc.rearrange("p (b d) -> p b d", d=64),
                                 axis=AX.X).then_inc(s_hr, 1)
            @b.tensor
            def _(t):
                nonlocal_first = first_sel
                for gi, ic in enumerate(gp):
                    t.wait_ge(s_hr, hr0 + gi + 1)
                    t.matmul(ctx.bank(7)[0:16, 0:16],
                             selA[:, ic * 16:(ic + 1) * 16],
                             hred[:, (gi % 2) * 16:(gi % 2) * 16 + 16],
                             start=(nonlocal_first and gi == 0),
                             stop=(gp is groups[-1] and gi == len(gp) - 1)
                             ).then_inc(s_hc, 1)
        ctx.cnt["la_hr"] += len(gp)
        ctx.cnt["la_hc"] += len(gp)
        first_sel = False
    den = ctx.sbuf("la_den", [16, 16], F32)
    bcol = ctx.sbuf("la_bcol", [16, 1], F32)
    with nc.Block() as b:
        @b.vector
        def _(v):
            v.tensor_copy(den[:], ctx.bank(7)[0:16, 0:16])
    with nc.Block() as b:
        @b.vector
        def _(v):
            v.reduce_max(arow[:, 0:1], den[:], axis=AX.X)
    with nc.Block() as b:
        @b.vector
        def _(v):
            v.tensor_scalar(bcol[:], arow[:, 0:1], -0.125, None, OP.mult)
    with nc.Block() as b:
        @b.scalar
        def _(s):
            s.activation(att[:], den[:], AF.Exp,
                         scale=0.125, bias=bcol[:])
    with nc.Block() as b:
        @b.vector
        def _(v):
            v.reduce_sum(arow[:, 1:2], att[:], axis=AX.X)
    with nc.Block() as b:
        @b.vector
        def _(v):
            v.reciprocal(arow[:, 2:3], arow[:, 1:2])
    with nc.Block() as b:
        @b.vector
        def _(v):
            v.tensor_tensor(att[:], att[:], arow[:, 2:3].broadcast_to((16, 16)),
                            OP.mult)
    if stages < 2:
        return dict(qtok_h=qtok_h, ktok_h=ktok_h, vfe_h=vfe_h, ofe_h=ofe_h)
    s_c1 = ctx.semaphore("la_c1")
    c10 = ctx.cnt["la_c1"]
    with nc.Block() as b:
        @b.tensor
        def _(t):
            t.matmul(ctx.bank(0)[0:16, :], att[:], rep16[:, 0:512],
                     start=True, stop=True)
            t.matmul(ctx.bank(1)[0:16, :], att[:], rep16[:, 512:1024],
                     start=True, stop=True).then_inc(s_c1, 1)
        @b.scalar
        def _(s):
            s.wait_ge(s_c1, c10 + 1)
            s.copy(atr[:], ctx.bank(0, 2)[0:16, :])
    ctx.cnt["la_c1"] += 1
    s_c2 = ctx.semaphore("la_c2")
    s_c3 = ctx.semaphore("la_c3")
    c20, c30 = ctx.cnt["la_c2"], ctx.cnt["la_c3"]
    with nc.Block() as b:
        @b.tensor
        def _(t):
            for kc in range(8):
                par = kc % 2
                if kc >= 2:
                    t.wait_ge(s_c3, c30 + kc - 1)
                t.matmul(ctx.bank(2 + 2 * par), hcsel[:, kc * 128:(kc + 1) * 128],
                         atr[:, 0:512], start=True, stop=True)
                t.matmul(ctx.bank(3 + 2 * par), hcsel[:, kc * 128:(kc + 1) * 128],
                         atr[:, 512:1024], start=True, stop=True).then_inc(s_c2, 1)
        @b.vector
        def _(v):
            for kc in range(8):
                par = kc % 2
                v.wait_ge(s_c2, c20 + kc + 1)
                v.tensor_tensor(bas[:, kc * 1024:(kc + 1) * 1024],
                                ctx.bank(2 + 2 * par, 2), maskI[:],
                                OP.mult).then_inc(s_c3, 1)
    ctx.cnt["la_c2"] += 8
    ctx.cnt["la_c3"] += 8
    if stages < 3:
        return dict(qtok_h=qtok_h, ktok_h=ktok_h, vfe_h=vfe_h, ofe_h=ofe_h)
    for c in range(NTC):
        t0 = c * TC
        with nc.Block() as b:
            @b.sync
            def _(sy):
                sy.wait_ge(s_st, ctx.cnt["la_st"])
                sy.dma_start(vf.ap().rearrange("p (fc t) -> p fc t", t=TC),
                             vfe_h.rearrange("(fc p) t -> p fc t", p=128)
                             [:, :, t0:t0 + TC]).then_inc(s_ld, 16)
        ctx.cnt["la_ld"] += 16
        s_dm = ctx.semaphore("la_dm")
        s_de = ctx.semaphore("la_de")
        dm0, de0 = ctx.cnt["la_dm"], ctx.cnt["la_de"]
        with nc.Block() as b:
            @b.tensor
            def _(t):
                t.wait_ge(s_ld, ctx.cnt["la_ld"])
                for ic in range(NF):
                    if ic >= 4:
                        t.wait_ge(s_de, de0 + ic - 3)
                    for kc in range(NF):
                        ins = t.matmul(
                            ps[:, (ic % 4) * 512:(ic % 4) * 512 + TC],
                            bas[:, kc * 1024 + ic * 128:kc * 1024 + (ic + 1) * 128],
                            vf[:, kc * TC:(kc + 1) * TC],
                            start=(kc == 0), stop=(kc == NF - 1))
                    ins.then_inc(s_dm, 1)
            @b.scalar
            def _(s):
                s.wait_ge(s_st, ctx.cnt["la_st"])
                for ic in range(NF):
                    s.wait_ge(s_dm, dm0 + ic + 1)
                    s.copy(qf[:, ic * TC:(ic + 1) * TC],
                           ps[:, (ic % 4) * 512:(ic % 4) * 512 + TC]
                           ).then_inc(s_de, 1)
        ctx.cnt["la_dm"] += NF
        ctx.cnt["la_de"] += NF
        with nc.Block() as b:
            @b.sync
            def _(sy):
                sy.dma_start(ofe_h.rearrange("(fc p) t -> p fc t", p=128)
                             [:, :, t0:t0 + TC],
                             qf.ap().rearrange("p (fc t) -> p fc t", t=TC)
                             ).then_inc(s_st, 16)
        ctx.cnt["la_st"] += 16
    if stages < 4:
        return dict(qtok_h=qtok_h, ktok_h=ktok_h, vfe_h=vfe_h, ofe_h=ofe_h)
    for a in range(16):
        with nc.Block() as b:
            @b.sync
            def _(sy):
                sy.wait_ge(s_st, ctx.cnt["la_st"])
                sy.dma_start(oa[0:64, :], ofe_h[a * 64:(a + 1) * 64, :]
                             ).then_inc(s_ld, 16)
                sy.dma_start(xc_r[:, :, 0:256],
                             xin_r[:, :, a * 256:(a + 1) * 256]).then_inc(s_ld, 16)
        ctx.cnt["la_ld"] += 32
        oa_r = oa.ap().rearrange("p (rr sm) -> p sm rr", sm=16)
        s_em = ctx.semaphore("la_em")
        em0 = ctx.cnt["la_em"]
        with nc.Block() as b:
            @b.tensor
            def _(t):
                t.wait_ge(s_ld, ctx.cnt["la_ld"])
                for oc in range(8):
                    for sm in range(16):
                        ins = t.matmul(
                            ctx.bank(oc)[:, 0:256],
                            wo[0:64,
                               sm * 1024 + oc * 128:sm * 1024 + (oc + 1) * 128],
                            oa_r[0:64, sm, :],
                            start=(sm == 0), stop=(sm == 15))
                    ins.then_inc(s_em, 1)
            @b.vector
            def _(v):
                v.wait_ge(s_st, ctx.cnt["la_st"])
                for oc in range(8):
                    v.wait_ge(s_em, em0 + oc + 1)
                    v.scalar_tensor_tensor(
                        xo_sb[:, oc * 256:(oc + 1) * 256],
                        ctx.bank(oc)[:, 0:256], bo[:, oc:oc + 1],
                        xc[:, oc * TC:oc * TC + 256], OP.add, OP.add)
        ctx.cnt["la_em"] += 8
        with nc.Block() as b:
            @b.sync
            def _(sy):
                sy.dma_start(xout_r[:, :, a * 256:(a + 1) * 256],
                             xo_sb.rearrange("p (fc t) -> p fc t", t=256)
                             ).then_inc(s_st, 16)
        ctx.cnt["la_st"] += 16
    return dict(qtok_h=qtok_h, ktok_h=ktok_h, vfe_h=vfe_h, ofe_h=ofe_h)


def prep_l1attn_weights(inp):
    f = np.ascontiguousarray
    W = {}
    W["wqT"] = f(inp["L1_attn_Wq"])
    W["wkT"] = f(inp["L1_attn_Wk"])
    W["wvT"] = f(inp["L1_attn_Wv"])
    W["woT"] = f(inp["L1_attn_Wo"])
    W["bo_r"] = f(inp["L1_attn_bo"].reshape(8, 128).T)
    W["ident"] = np.eye(128, dtype=np.float32)
    mask = np.zeros((128, 1024), np.float32)
    for p in range(128):
        for j in range(p % 64, 1024, 64):
            mask[p, j] = 1.0
    W["maskI"] = mask
    rep = np.zeros((16, 1024), np.float32)
    for aa in range(16):
        rep[aa, aa * 64:(aa + 1) * 64] = 1.0
    W["rep16"] = rep
    hc = np.zeros((16, 8 * 128), np.float32)
    for kc in range(8):
        for p in range(128):
            hc[2 * kc + p // 64, kc * 128 + p] = 1.0
    W["hcsel"] = hc
    sa = np.zeros((128, 8 * 16), np.float32)
    for ic in range(8):
        for p in range(128):
            sa[p, ic * 16 + 2 * ic + p // 64] = 1.0
    W["selA"] = sa
    return W


LA_SHAPES = dict(
    wqT=(1024, 1024), wkT=(1024, 1024), wvT=(1024, 1024), woT=(1024, 1024),
    bo_r=(128, 8), ident=(128, 128), maskI=(128, 1024), rep16=(16, 1024),
    hcsel=(16, 1024), selA=(128, 128))


def build_l1attn_program(debug=False, stages=4):
    nc = bass.Bass("TRN2", target_bir_lowering=False)
    W = {k: nc.dram_tensor("i_" + k, list(s), F32, kind="ExternalInput").ap()
         for k, s in LA_SHAPES.items()}
    xin = nc.dram_tensor("x", [1024, 4096], F32, kind="ExternalInput").ap()
    xout = nc.dram_tensor("xo", [1024, 4096], F32, kind="ExternalOutput").ap()
    st = contextlib.ExitStack()
    with st:
        ctx = Ctx(nc, st)
        alloc_common(ctx)
        emitted = emit_l1attn(ctx, xin, xout, W, stages=stages)
        hs = emitted
        if debug:
            d_att = nc.dram_tensor("d_att", [16, 16], F32, kind="ExternalOutput").ap()
            d_en = nc.dram_tensor("d_en", [16, 16], F32, kind="ExternalOutput").ap()
            d_ar = nc.dram_tensor("d_ar", [16, 3], F32, kind="ExternalOutput").ap()
            d_atr = nc.dram_tensor("d_atr", [16, 1024], F32, kind="ExternalOutput").ap()
            d_qt = nc.dram_tensor("d_qt", [128, 1024], F32, kind="ExternalOutput").ap()
            d_of = nc.dram_tensor("d_of", [128, 4096], F32, kind="ExternalOutput").ap()
            with ctx.nc.Block() as b:
                @b.gpsimd
                def _(g):
                    g.wait_ge(ctx.sem["la_st"], ctx.cnt["la_st"])
                    g.dma_start(d_att, ctx.t["la_att"][:]).then_inc(ctx.sem["la_st"], 16)
                    g.dma_start(d_en, ctx.t["la_den"][:]).then_inc(ctx.sem["la_st"], 16)
                    g.dma_start(d_ar, ctx.t["la_arow"][:]).then_inc(ctx.sem["la_st"], 16)
                    g.dma_start(d_atr, ctx.t["la_atr"][:]).then_inc(ctx.sem["la_st"], 16)
                    g.dma_start(d_qt, hs["qtok_h"][512:640, :]).then_inc(ctx.sem["la_st"], 16)
                    g.dma_start(d_of, hs["ofe_h"][0:128, :]).then_inc(ctx.sem["la_st"], 16)
            ctx.cnt["la_st"] += 96
        finish(ctx, ["la_st"])
    return nc


def emit_moe(ctx, xin, xout, W):
    nc = ctx.nc
    ps = ctx.t["ps"]
    y = ctx.sbuf("mo_y", [128, 8 * 4096], BF16)
    gu = ctx.sbuf("mo_gu", [128, 16 * 1024], BF16)
    wd_sb = ctx.sbuf("mo_wd", [128, 16 * 1024], BF16)
    wgu = ctx.sbuf("mo_wgu", [128, 4 * 1024], BF16)
    wr_sb = ctx.sbuf("mo_wr", [128, 8 * 8], BF16)
    weB = ctx.sbuf("mo_web", [128, 1024], BF16)
    rr0 = ctx.sbuf("mo_rr0", [128, 256], F32)
    rr1 = ctx.sbuf("mo_rr1", [128, 256], F32)
    rr2 = ctx.sbuf("mo_rr2", [128, 256], F32)
    rr3 = ctx.t["ln_sq"].ap().bitcast(F32)[:, 0:256]
    rr4 = ctx.t["ln_apply_scr"][:, 0:256]
    rows = ctx.sbuf("mo_rows", [128, 32 * 9], F32)
    w8 = ctx.sbuf("mo_w8", [8, 4096], F32)
    oh8 = ctx.sbuf("mo_oh8", [8, 8 * 128], F32)
    idf = ctx.sbuf("mo_idf", [128, 128], F32)
    gsil = ctx.sbuf("mo_gsil", [128, 2 * 512], F32)
    tmp = gsil
    xcm = ctx.sbuf("mo_xcm", [128, 4096], F32)
    acc_h = nc.dram_tensor("mo_acc", [1024, 4096], F32, kind="Internal").ap()

    def mrow(i):
        return rows[:, 32 * i:32 * (i + 1)]
    M1E, SUME, M2E, RSUM, S1, S2, DEN, W1C, W2C = range(9)

    s_ld = ctx.semaphore("mo_ld")
    with nc.Block() as b:
        @b.sync
        def _(sy):
            sy.dma_start(oh8[:], W["oh8"]).then_inc(s_ld, 16)
            sy.dma_start(idf[:], W["ident"]).then_inc(s_ld, 16)
        @b.gpsimd
        def _(g):
            g.dma_start(wr_sb.ap().rearrange("p (kc e) -> p kc e", kc=8),
                        W["wr"].rearrange("(kc p) e -> p kc e", p=128)
                        ).then_inc(s_ld, 16)
    ctx.cnt["mo_ld"] += 48

    xin_r = xin.rearrange("(fc p) t -> p fc t", p=128)
    xout_r = xout.rearrange("(fc p) t -> p fc t", p=128)
    xcm_r = xcm.ap().rearrange("p (fc t) -> p fc t", t=TC)
    s_st = ctx.semaphore("mo_st")
    for c in range(NTC):
        t0 = c * TC
        with nc.Block() as b:
            @b.sync
            def _(sy):
                sy.dma_start(xcm_r, xin_r[:, :, t0:t0 + TC]).then_inc(s_ld, 16)
        ctx.cnt["mo_ld"] += 16
        x_tiles = [xcm[:, fc * TC:(fc + 1) * TC] for fc in range(NF)]
        y_tiles = [y[:, fc * 4096 + t0:fc * 4096 + t0 + TC] for fc in range(NF)]
        emit_ln(ctx, x_tiles, y_tiles, TC, double=False, pre_wait="mo_ld")
    s_rm = ctx.semaphore("mo_rm")
    s_re = ctx.semaphore("mo_re")
    rm0, re0 = ctx.cnt["mo_rm"], ctx.cnt["mo_re"]
    with nc.Block() as b:
        @b.tensor
        def _(t):
            for ch in range(32):
                if ch >= 4:
                    t.wait_ge(s_re, re0 + ch - 3)
                for fc in range(NF):
                    ins = t.matmul(ctx.bank(ch % 4)[:, 0:8],
                                   y[:, fc * 4096 + ch * 128:fc * 4096 + (ch + 1) * 128],
                                   wr_sb[:, fc * 8:(fc + 1) * 8],
                                   start=(fc == 0), stop=(fc == NF - 1))
                ins.then_inc(s_rm, 1)
        @b.scalar
        def _(s):
            for ch in range(32):
                s.wait_ge(s_rm, rm0 + ch + 1)
                s.copy(rr0[:, ch * 8:(ch + 1) * 8],
                       ctx.bank(ch % 4)[:, 0:8]).then_inc(s_re, 1)
    ctx.cnt["mo_rm"] += 32
    ctx.cnt["mo_re"] += 32
    rr0_3 = rr0.ap().rearrange("p (ch e) -> p ch e", e=8)
    rr1_3 = rr1.ap().rearrange("p (ch e) -> p ch e", e=8)
    rr2_3 = rr2.ap().rearrange("p (ch e) -> p ch e", e=8)
    rr3_3 = rr3.rearrange("p (ch e) -> p ch e", e=8)
    rr4_3 = rr4.rearrange("p (ch e) -> p ch e", e=8)
    def bc(col):
        return col.unsqueeze(2).broadcast_to((128, 32, 8))
    with nc.Block() as b:
        @b.vector
        def _(v):
            v.reduce_max(mrow(M1E).unsqueeze(2), rr0_3, axis=AX.X)
    with nc.Block() as b:
        @b.vector
        def _(v):
            v.tensor_tensor(rr1_3, rr0_3, bc(mrow(M1E)), OP.subtract)
    with nc.Block() as b:
        @b.scalar
        def _(s):
            s.activation(rr1[:], rr1[:], AF.Exp)
    with nc.Block() as b:
        @b.vector
        def _(v):
            v.reduce_sum(mrow(SUME).unsqueeze(2), rr1_3, axis=AX.X)
            v.reduce_max(mrow(M1E).unsqueeze(2), rr1_3, axis=AX.X)
    with nc.Block() as b:
        @b.vector
        def _(v):
            v.tensor_tensor(rr2_3, rr1_3, bc(mrow(M1E)), OP.is_equal)
    with nc.Block() as b:
        @b.vector
        def _(v):
            v.tensor_scalar(rr3, rr2[:], -1.0, 1.0, OP.mult, OP.add)
    with nc.Block() as b:
        @b.vector
        def _(v):
            v.tensor_tensor(rr3, rr1[:], rr3, OP.mult)
    with nc.Block() as b:
        @b.vector
        def _(v):
            v.reduce_max(mrow(M2E).unsqueeze(2), rr3_3, axis=AX.X)
            v.reciprocal(mrow(RSUM), mrow(SUME))
    with nc.Block() as b:
        @b.vector
        def _(v):
            v.tensor_tensor(rr3_3, rr3_3, bc(mrow(M2E)), OP.is_equal)
            v.tensor_tensor(mrow(S1), mrow(M1E), mrow(RSUM), OP.mult)
            v.tensor_tensor(mrow(S2), mrow(M2E), mrow(RSUM), OP.mult)
    with nc.Block() as b:
        @b.vector
        def _(v):
            v.scalar_tensor_tensor(mrow(DEN), mrow(S1), 1e-20, mrow(S2),
                                   OP.add, OP.add)
    with nc.Block() as b:
        @b.vector
        def _(v):
            v.reciprocal(mrow(DEN), mrow(DEN))
    with nc.Block() as b:
        @b.vector
        def _(v):
            v.tensor_tensor(mrow(W1C), mrow(S1), mrow(DEN), OP.mult)
            v.tensor_tensor(mrow(W2C), mrow(S2), mrow(DEN), OP.mult)
    with nc.Block() as b:
        @b.vector
        def _(v):
            v.tensor_tensor(rr4_3, rr2_3, bc(mrow(W1C)), OP.mult)
    with nc.Block() as b:
        @b.vector
        def _(v):
            v.tensor_tensor(rr3_3, rr3_3, bc(mrow(W2C)), OP.mult)
    with nc.Block() as b:
        @b.vector
        def _(v):
            v.tensor_tensor(rr4, rr4, rr3, OP.add)
    s_tw = ctx.semaphore("mo_tw")
    s_te = ctx.semaphore("mo_te")
    tw0, te0 = ctx.cnt["mo_tw"], ctx.cnt["mo_te"]
    with nc.Block() as b:
        @b.tensor
        def _(t):
            for ch in range(32):
                if ch >= 4:
                    t.wait_ge(s_te, te0 + ch - 3)
                t.transpose(ctx.bank(ch % 4)[0:8, 0:128],
                            rr4[0:128, ch * 8:(ch + 1) * 8], idf[:]
                            ).then_inc(s_tw, 1)
        @b.scalar
        def _(s):
            for ch in range(32):
                s.wait_ge(s_tw, tw0 + ch + 1)
                s.copy(w8[:, ch * 128:(ch + 1) * 128],
                       ctx.bank(ch % 4)[0:8, 0:128]).then_inc(s_te, 1)
    ctx.cnt["mo_tw"] += 32
    ctx.cnt["mo_te"] += 32
    s_wde = ctx.semaphore("mo_wde")
    s_wb = ctx.semaphore("mo_wb")
    s_gld = ctx.semaphore("mo_gld")
    s_p1 = ctx.semaphore("mo_p1")
    s_sil = ctx.semaphore("mo_sil")
    s_guv = ctx.semaphore("mo_guv")
    s_p2 = ctx.semaphore("mo_p2")
    s_ev2 = ctx.semaphore("mo_ev2")
    s_acd = ctx.semaphore("mo_acd")
    for e in range(9):
        shared = (e == 8)
        wg_src = W["sg"] if shared else W["wg"][e]
        wu_src = W["su"] if shared else W["wu"][e]
        wd_src = W["sd"] if shared else W["wd"][e]
        with nc.Block() as b:
            @b.gpsimd
            def _(g):
                g.dma_start(wd_sb.ap().rearrange("p (kc i) -> p kc i", kc=16),
                            wd_src.rearrange("(kc p) i -> p kc i", p=128)
                            ).then_inc(s_wde, 16)
        ctx.cnt["mo_wde"] += 16
        wb0 = ctx.cnt["mo_wb"]
        with nc.Block() as b:
            if shared:
                @b.vector
                def _(v):
                    v.memset(weB[:], 1.0)
            else:
                @b.tensor
                def _(t):
                    t.matmul(ctx.bank(0), oh8[:, e * 128:(e + 1) * 128],
                             w8[:, 0:2048].rearrange("p (j t) -> p j t", j=4)[:, 0, :] if False else w8[:, 0:512],
                             start=True, stop=True)
                    t.matmul(ctx.bank(1), oh8[:, e * 128:(e + 1) * 128],
                             w8[:, 512:1024], start=True, stop=True).then_inc(s_wb, 1)
                @b.scalar
                def _(s):
                    s.wait_ge(s_wb, wb0 + 1)
                    s.copy(weB[:, 0:512], ctx.bank(0))
                    s.copy(weB[:, 512:1024], ctx.bank(1))
        if not shared:
            ctx.cnt["mo_wb"] += 1
        for q in range(4):
            tq0 = q * 1024
            if not shared:
                wb0 = ctx.cnt["mo_wb"]
                with nc.Block() as b:
                    @b.tensor
                    def _(t):
                        t.matmul(ctx.bank(0), oh8[:, e * 128:(e + 1) * 128],
                                 w8[:, tq0:tq0 + 512], start=True, stop=True)
                        t.matmul(ctx.bank(1), oh8[:, e * 128:(e + 1) * 128],
                                 w8[:, tq0 + 512:tq0 + 1024],
                                 start=True, stop=True).then_inc(s_wb, 1)
                    @b.scalar
                    def _(s):
                        s.wait_ge(s_wb, wb0 + 1)
                        s.copy(weB[:, 0:512], ctx.bank(0))
                        s.copy(weB[:, 512:1024], ctx.bank(1))
                ctx.cnt["mo_wb"] += 1
            gld0, p10 = ctx.cnt["mo_gld"], ctx.cnt["mo_p1"]
            sil0, guv0 = ctx.cnt["mo_sil"], ctx.cnt["mo_guv"]
            with nc.Block() as b:
                @b.gpsimd
                def _(g):
                    for ic in range(16):
                        par = ic % 2
                        if ic >= 2:
                            g.wait_ge(s_p1, p10 + 2 * (ic - 1))
                        g.dma_start(
                            wgu.ap().rearrange("p (s kc i) -> p s kc i", s=4, i=128)
                            [:, 2 * par],
                            wg_src[:, ic * 128:(ic + 1) * 128]
                            .rearrange("(kc p) i -> p kc i", p=128)
                            ).then_inc(s_gld, 16)
                        g.dma_start(
                            wgu.ap().rearrange("p (s kc i) -> p s kc i", s=4, i=128)
                            [:, 2 * par + 1],
                            wu_src[:, ic * 128:(ic + 1) * 128]
                            .rearrange("(kc p) i -> p kc i", p=128)
                            ).then_inc(s_gld, 16)
                @b.tensor
                def _(t):
                    for ic in range(16):
                        par = ic % 2
                        t.wait_ge(s_gld, gld0 + 32 * (ic + 1))
                        for jq in range(2):
                            idx = 2 * ic + jq
                            if idx >= 2:
                                t.wait_ge(s_guv, guv0 + idx - 1)
                            for kc in range(NF):
                                t.matmul(ctx.bank(2 * (idx % 2)),
                                         wgu[:, 2 * par * 1024 + kc * 128:
                                             2 * par * 1024 + (kc + 1) * 128],
                                         y[:, kc * 4096 + tq0 + jq * 512:
                                           kc * 4096 + tq0 + (jq + 1) * 512],
                                         start=(kc == 0), stop=(kc == NF - 1))
                            for kc in range(NF):
                                ins = t.matmul(
                                    ctx.bank(2 * (idx % 2) + 1),
                                    wgu[:, (2 * par + 1) * 1024 + kc * 128:
                                        (2 * par + 1) * 1024 + (kc + 1) * 128],
                                    y[:, kc * 4096 + tq0 + jq * 512:
                                      kc * 4096 + tq0 + (jq + 1) * 512],
                                    start=(kc == 0), stop=(kc == NF - 1))
                            ins.then_inc(s_p1, 1)
                @b.scalar
                def _(s):
                    s.wait_ge(s_acd, ctx.cnt["mo_acd"])
                    for ic in range(16):
                        for jq in range(2):
                            idx = 2 * ic + jq
                            s.wait_ge(s_p1, p10 + idx + 1)
                            s.activation(gsil[:, (idx % 2) * 512:(idx % 2) * 512 + 512],
                                         ctx.bank(2 * (idx % 2)), AF.Silu
                                         ).then_inc(s_sil, 1)
                @b.vector
                def _(v):
                    for ic in range(16):
                        for jq in range(2):
                            idx = 2 * ic + jq
                            v.wait_ge(s_sil, sil0 + idx + 1)
                            v.tensor_tensor(
                                gu[:, ic * 1024 + jq * 512:ic * 1024 + (jq + 1) * 512],
                                gsil[:, (idx % 2) * 512:(idx % 2) * 512 + 512],
                                ctx.bank(2 * (idx % 2) + 1),
                                OP.mult).then_inc(s_guv, 1)
            ctx.cnt["mo_gld"] += 32 * 16
            ctx.cnt["mo_p1"] += 32
            ctx.cnt["mo_sil"] += 32
            ctx.cnt["mo_guv"] += 32
            p20, ev20, acd0 = (ctx.cnt["mo_p2"], ctx.cnt["mo_ev2"],
                               ctx.cnt["mo_acd"])
            with nc.Block() as b:
                @b.tensor
                def _(t):
                    t.wait_ge(s_wde, ctx.cnt["mo_wde"])
                    for dm in range(8):
                        for jq in range(2):
                            idx = 2 * dm + jq
                            if idx >= 2:
                                t.wait_ge(s_ev2, ev20 + idx - 1)
                            for kc in range(16):
                                ins = t.matmul(
                                    ctx.bank(4 + idx % 2),
                                    wd_sb[:, kc * 1024 + dm * 128:
                                          kc * 1024 + (dm + 1) * 128],
                                    gu[:, kc * 1024 + jq * 512:
                                       kc * 1024 + (jq + 1) * 512],
                                    start=(kc == 0), stop=(kc == 15))
                            ins.then_inc(s_p2, 1)
                @b.vector
                def _(v):
                    for dm in range(8):
                        for jq in range(2):
                            idx = 2 * dm + jq
                            v.wait_ge(s_p2, p20 + idx + 1)
                            if idx >= 2:
                                v.wait_ge(s_acd, acd0 + 16 * (idx - 1))
                            v.tensor_tensor(
                                tmp[:, (idx % 2) * 512:(idx % 2) * 512 + 512],
                                ctx.bank(4 + idx % 2),
                                weB[:, jq * 512:(jq + 1) * 512],
                                OP.mult).then_inc(s_ev2, 1)
                @b.gpsimd
                def _(g):
                    for dm in range(8):
                        for jq in range(2):
                            idx = 2 * dm + jq
                            g.wait_ge(s_ev2, ev20 + idx + 1)
                            g.dma_start(
                                acc_h[dm * 128:(dm + 1) * 128,
                                      tq0 + jq * 512:tq0 + (jq + 1) * 512],
                                tmp[:, (idx % 2) * 512:(idx % 2) * 512 + 512],
                                accum_op=(OP.bypass if e == 0 else OP.add)
                                ).then_inc(s_acd, 16)
            ctx.cnt["mo_p2"] += 16
            ctx.cnt["mo_ev2"] += 16
            ctx.cnt["mo_acd"] += 16 * 16
    guf = gu.ap().bitcast(F32)
    guf_r = guf[:, 0:4096].rearrange("p (fc t) -> p fc t", t=TC)
    for c in range(NTC):
        t0 = c * TC
        with nc.Block() as b:
            @b.sync
            def _(sy):
                sy.wait_ge(ctx.sem["mo_acd"], ctx.cnt["mo_acd"])
                sy.wait_ge(s_st, ctx.cnt["mo_st"])
                sy.dma_start(xcm_r, xin_r[:, :, t0:t0 + TC]).then_inc(s_ld, 16)
                sy.dma_start(guf_r,
                             acc_h.rearrange("(fc p) t -> p fc t", p=128)
                             [:, :, t0:t0 + TC]).then_inc(s_ld, 16)
        ctx.cnt["mo_ld"] += 32
        with nc.Block() as b:
            @b.vector
            def _(v):
                v.wait_ge(ctx.sem["mo_ld"], ctx.cnt["mo_ld"])
                for fc in range(NF):
                    v.tensor_tensor(xcm[:, fc * TC:(fc + 1) * TC],
                                    xcm[:, fc * TC:(fc + 1) * TC],
                                    guf[:, fc * TC * 8 // 8:0] if False else
                                    guf[:, 0:4096][:, fc * TC:(fc + 1) * TC],
                                    OP.add)
        with nc.Block() as b:
            @b.sync
            def _(sy):
                sy.dma_start(xout_r[:, :, t0:t0 + TC], xcm_r).then_inc(s_st, 16)
        ctx.cnt["mo_st"] += 16


def prep_moe_weights(inp):
    f = np.ascontiguousarray
    W = {}
    W["wr"] = f(inp["L1_ffn_Wr"])
    W["wg"] = f(inp["L1_ffn_Wg"])
    W["wu"] = f(inp["L1_ffn_Wu"])
    W["wd"] = f(inp["L1_ffn_Wd"])
    W["sg"] = f(inp["L1_ffn_Sg"])
    W["su"] = f(inp["L1_ffn_Su"])
    W["sd"] = f(inp["L1_ffn_Sd"])
    oh = np.zeros((8, 8 * 128), np.float32)
    for e in range(8):
        oh[e, e * 128:(e + 1) * 128] = 1.0
    W["oh8"] = oh
    W["ident"] = np.eye(128, dtype=np.float32)
    return W


MO_SHAPES = dict(wr=(1024, 8), wg=(8, 1024, 2048), wu=(8, 1024, 2048),
                 wd=(8, 2048, 1024), sg=(1024, 2048), su=(1024, 2048),
                 sd=(2048, 1024), oh8=(8, 1024), ident=(128, 128))


def build_moe_program():
    nc = bass.Bass("TRN2", target_bir_lowering=False)
    W = {k: nc.dram_tensor("i_" + k, list(s), F32, kind="ExternalInput").ap()
         for k, s in MO_SHAPES.items()}
    xin = nc.dram_tensor("x", [1024, 4096], F32, kind="ExternalInput").ap()
    xout = nc.dram_tensor("xo", [1024, 4096], F32, kind="ExternalOutput").ap()
    st = contextlib.ExitStack()
    with st:
        ctx = Ctx(nc, st)
        alloc_common(ctx)
        emit_moe(ctx, xin, xout, W)
        finish(ctx, ["mo_st"])
    return nc


NA = 22


def emit_lformer(ctx, xin, xout, W, nlayers=2, attn_upto=9, do_ffn=True):
    nc = ctx.nc
    ps = ctx.t["ps"]
    wqkv = ctx.sbuf("lf_wqkv", [128, 8 * 1536], BF16)
    wo = ctx.sbuf("lf_wo", [128, 4 * 1024], BF16)
    qscol = ctx.sbuf("lf_qs", [128, 2], F32)
    kscol = ctx.sbuf("lf_ks", [128, 2], F32)
    lgc = ctx.sbuf("lf_lg", [128, 16], F32)
    lbc = ctx.sbuf("lf_lb", [128, 16], F32)
    b1ac = ctx.sbuf("lf_b1a", [128, 2 * NA], F32)
    b1gc = ctx.sbuf("lf_b1g", [128, 2 * NA], F32)
    b2c = ctx.sbuf("lf_b2", [128, 16], F32)
    blk64 = ctx.sbuf("lf_blk64", [128, 128], F32)
    rott = ctx.sbuf("lf_rott", [128, 128], BF16)
    idf2 = ctx.sbuf("lf_idf", [128, 128], F32)
    mb = ctx.sbuf("lf_mb", [128, 8 * 1024], BF16)
    mbt = ctx.sbuf("lf_mbt", [128, 9 * 512], BF16)
    rot4 = ctx.sbuf("lf_rot4", [128, 4 * 512], F32)
    xlc = ctx.sbuf("lf_xlc", [128, 8 * 512], F32)
    xs = ctx.sbuf("lf_xs", [128, 8 * 512], BF16)
    qf = ctx.sbuf("lf_qf", [128, 4 * 512], BF16)
    kf2 = ctx.sbuf("lf_kf2", [128, 2 * 4 * 512], BF16)
    vt2 = ctx.sbuf("lf_vt2", [128, 2 * 4 * 512], BF16)
    o_all = ctx.sbuf("lf_oall", [128, 4 * 512], BF16)
    sa = ctx.sbuf("lf_sa", [128, 1024], F32)
    pb = ctx.sbuf("lf_pb", [128, 2 * 512], F32)
    pT = ctx.sbuf("lf_pT", [128, 2 * 512], BF16)
    m8B = ctx.sbuf("lf_m8b", [128, 512], F32)
    rcpB = ctx.sbuf("lf_rcpb", [128, 512], F32)
    mx4 = ctx.sbuf("lf_mx4", [128, 8], F32)
    rsr = ctx.sbuf("lf_rsr", [1, 512], F32)
    gu_tc = ctx.sbuf("lf_gu", [128, NA * 512], BF16)
    w1b = ctx.sbuf("lf_w1b", [128, 4 * 1024], BF16)
    w2b = ctx.sbuf("lf_w2b", [128, 2 * 1024], BF16)
    mcol1 = ctx.sbuf("lf_mc1", [128, 16], F32)
    mcol2 = ctx.sbuf("lf_mc2", [128, 16], F32)
    sq2 = ctx.sbuf("lf_sq2", [128, 512], F32)
    gsl = ctx.sbuf("lf_gsl", [128, 2 * 512], F32)

    xl_h = nc.dram_tensor("lf_xl", [1024, 4096], F32, kind="Internal").ap()
    xl2_h = nc.dram_tensor("lf_xl2", [1024, 4096], F32, kind="Internal").ap()
    w1a_h = nc.dram_tensor("lf_w1a", [2, 1024, NA * 128], BF16, kind="Internal").ap()
    w1g_h = nc.dram_tensor("lf_w1g", [2, 1024, NA * 128], BF16, kind="Internal").ap()
    w2_h = nc.dram_tensor("lf_w2h", [2, NA * 128, 1024], BF16, kind="Internal").ap()

    s_ld = ctx.semaphore("lf_ld")
    s_st = ctx.semaphore("lf_st")
    pairs = [(qscol[:], W["qs_r"]), (kscol[:], W["ks_r"]),
             (lgc[:], W["lg_r"]), (lbc[:], W["lb_r"]),
             (b1ac[:], W["b1a_r"]), (b1gc[:], W["b1g_r"]), (b2c[:], W["b2_r"]),
             (blk64[:], W["blk64"]), (idf2[:], W["ident"])]
    casts = [(rott[:], W["rott"]), (mb[:], W["mb"]), (mbt[:], W["mbt"])]
    with nc.Block() as b:
        @b.sync
        def _(sy):
            for dst, src in pairs:
                sy.dma_start(dst, src).then_inc(s_ld, 16)
        @b.gpsimd
        def _(g):
            for dst, src in casts:
                g.dma_start(dst, src).then_inc(s_ld, 16)
    ctx.cnt["lf_ld"] += 16 * (len(pairs) + len(casts))
    stg = gu_tc
    for l in range(2):
        for part, src_h, dst_h, ncol in (
                ("a", W["w1a"], w1a_h, NA * 128), ("g", W["w1g"], w1g_h, NA * 128),
                ("w2", W["w2p"], w2_h, 1024)):
            nkc = 8 if part != "w2" else NA
            for kc in range(nkc):
                par = kc % 2
                sem = ctx.semaphore("lf_stg")
                base = ctx.cnt["lf_stg"]
                with nc.Block() as b:
                    @b.gpsimd
                    def _(g):
                        if base >= 32:
                            g.wait_ge(sem, base - 32 + 32)
                        g.dma_start(stg[:, par * 2816:par * 2816 + ncol],
                                    src_h[l, kc * 128:(kc + 1) * 128, :]
                                    ).then_inc(sem, 16)
                    @b.sync
                    def _(sy):
                        sy.wait_ge(sem, base + 16)
                        sy.dma_start(dst_h[l, kc * 128:(kc + 1) * 128, :],
                                     stg[:, par * 2816:par * 2816 + ncol]
                                     ).then_inc(sem, 16)
                ctx.cnt["lf_stg"] += 32
    with nc.Block() as b:
        @b.sync
        def _(sy):
            sy.wait_ge(ctx.sem["lf_stg"], ctx.cnt["lf_stg"])

    xin_r = xin.rearrange("(fc p) t -> p fc t", p=128)
    xout_r = xout.rearrange("(fc p) t -> p fc t", p=128)
    xlc_r = xlc.ap().rearrange("p (fc t) -> p fc t", t=TC)
    for c in range(NTC):
        t0 = c * TC
        with nc.Block() as b:
            @b.sync
            def _(sy):
                sy.wait_ge(s_st, ctx.cnt["lf_st"])
                sy.dma_start(xlc_r, xin_r[:, :, t0:t0 + TC]).then_inc(s_ld, 16)
        ctx.cnt["lf_ld"] += 16
        x_tiles = [xlc[:, fc * TC:(fc + 1) * TC] for fc in range(NF)]
        emit_ln(ctx, x_tiles, x_tiles, TC, double=False, pre_wait="lf_ld")
        with nc.Block() as b:
            @b.sync
            def _(sy):
                sy.dma_start(xl_h.rearrange("(fc p) t -> p fc t", p=128)
                             [:, :, t0:t0 + TC], xlc_r).then_inc(s_st, 16)
        ctx.cnt["lf_st"] += 16

    for l in range(nlayers):
        lin_h, lout_h = (xl_h, xl2_h) if l == 0 else (xl2_h, xl_h)
        with nc.Block() as b:
            @b.gpsimd
            def _(g):
                g.dma_start(wqkv.ap().rearrange("p (kc i) -> p kc i", kc=8),
                            W["wqkv"][l].rearrange("(kc p) i -> p kc i", p=128)
                            ).then_inc(s_ld, 16)
                g.dma_start(wo.ap().rearrange("p (kc i) -> p kc i", kc=4),
                            W["wof"][l].rearrange("(kc p) i -> p kc i", p=128)
                            ).then_inc(s_ld, 16)
            @b.vector
            def _(v):
                v.memset(kf2[:, 2048:4096], 0.0)
                v.memset(vt2[:, 2048:4096], 0.0)
        ctx.cnt["lf_ld"] += 32
        for c in range(NTC):
            t0 = c * TC
            cur, prv = c % 2, (c + 1) % 2
            with nc.Block() as b:
                @b.sync
                def _(sy):
                    sy.wait_ge(s_st, ctx.cnt["lf_st"])
                    sy.dma_start(xlc_r, lin_h.rearrange("(fc p) t -> p fc t", p=128)
                                 [:, :, t0:t0 + TC]).then_inc(s_ld, 16)
                    sy.dma_start(rot4.ap().rearrange("p (r t) -> p r t", r=4),
                                 W["rot4"][:, :, t0:t0 + TC]).then_inc(s_ld, 16)
            ctx.cnt["lf_ld"] += 32
            with nc.Block() as b:
                @b.vector
                def _(v):
                    v.wait_ge(ctx.sem["lf_ld"], ctx.cnt["lf_ld"])
                    for fc in range(NF):
                        v.tensor_copy(mcol1[:, 8 * (cur) + fc:8 * cur + fc + 1],
                                      xlc[:, fc * TC + 511:fc * TC + 512])
                    for fc in range(4):
                        v.tensor_copy(xs[:, fc * TC:(fc + 1) * TC],
                                      xlc[:, fc * TC:(fc + 1) * TC])
                    for fc in range(4, NF):
                        v.tensor_copy(xs[:, fc * TC + 1:(fc + 1) * TC],
                                      xlc[:, fc * TC:fc * TC + 511])
                        if c == 0:
                            v.memset(xs[:, fc * TC:fc * TC + 1], 0.0)
                        else:
                            v.tensor_copy(xs[:, fc * TC:fc * TC + 1],
                                          mcol1[:, 8 * prv + fc:8 * prv + fc + 1])
            emit_proj(ctx, "lf_q", wqkv, xs, qf, 4, NF, w_col=1536)
            emit_proj(ctx, "lf_k", wqkv, xs,
                      kf2[:, cur * 2048:cur * 2048 + 2048], 4, NF, w_col=1536,
                      ic_off=4)
            s_vm = ctx.semaphore("lf_vm")
            s_ve = ctx.semaphore("lf_ve")
            vm0, ve0 = ctx.cnt["lf_vm"], ctx.cnt["lf_ve"]
            with nc.Block() as b:
                @b.tensor
                def _(t):
                    for ts_ in range(4):
                        if ts_ >= 4:
                            t.wait_ge(s_ve, ve0 + ts_ - 3)
                        for kc in range(NF):
                            ins = t.matmul(
                                ps[:, (ts_ % 4) * 512:(ts_ % 4) * 512 + 512],
                                xs[:, kc * TC + ts_ * 128:kc * TC + (ts_ + 1) * 128],
                                wqkv[:, kc * 1536 + 1024:kc * 1536 + 1536],
                                start=(kc == 0), stop=(kc == NF - 1))
                        ins.then_inc(s_vm, 1)
                @b.scalar
                def _(s):
                    for ts_ in range(4):
                        s.wait_ge(s_vm, vm0 + ts_ + 1)
                        s.copy(vt2[:, cur * 2048 + ts_ * 512:cur * 2048 + (ts_ + 1) * 512],
                               ps[:, (ts_ % 4) * 512:(ts_ % 4) * 512 + 512]
                               ).then_inc(s_ve, 1)
            ctx.cnt["lf_vm"] += 4
            ctx.cnt["lf_ve"] += 4
            for which in (("q", "k") if attn_upto >= 2 else ()):
                src = qf if which == "q" else kf2
                off = 0 if which == "q" else cur * 2048
                scol = qscol if which == "q" else kscol
                ci, si_ = (0, 1) if which == "q" else (2, 3)
                s_l2 = ctx.semaphore("lf_l2")
                s_l3 = ctx.semaphore("lf_l3")
                s_l4 = ctx.semaphore("lf_l4")
                s_l5 = ctx.semaphore("lf_l5")
                for ft in range(4):
                    l20, l30, l40, l50 = (ctx.cnt["lf_l2"], ctx.cnt["lf_l3"],
                                          ctx.cnt["lf_l4"], ctx.cnt["lf_l5"])
                    tile = src[:, off + ft * 512:off + (ft + 1) * 512]
                    with nc.Block() as b:
                        @b.scalar
                        def _(s):
                            s.activation(sq2[:], tile, AF.Square).then_inc(s_l2, 1)
                            s.wait_ge(s_l3, l30 + 1)
                            s.activation(gsl[:, 0:512], ctx.bank(6), AF.Sqrt
                                         ).then_inc(s_l4, 1)
                        @b.tensor
                        def _(t):
                            t.wait_ge(s_l2, l20 + 1)
                            t.matmul(ctx.bank(6), blk64[:], sq2[:],
                                     start=True, stop=True).then_inc(s_l3, 1)
                            t.wait_ge(s_l5, l50 + 1)
                            t.matmul(ctx.bank(7), rott[:],
                                     pT[:, 0:512], start=True, stop=True
                                     ).then_inc(s_l3, 1)
                        @b.vector
                        def _(v):
                            v.wait_ge(s_l4, l40 + 1)
                            v.tensor_scalar_max(gsl[:, 0:512], gsl[:, 0:512], 1e-12)
                            v.reciprocal(gsl[:, 0:512], gsl[:, 0:512])
                            v.tensor_scalar(gsl[:, 512:1024], gsl[:, 0:512],
                                            scol[:, l:l + 1], None, OP.mult)
                            v.tensor_tensor(pT[:, 0:512], tile,
                                            gsl[:, 512:1024], OP.mult
                                            ).then_inc(s_l5, 1)
                            v.wait_ge(s_l3, l30 + 2)
                            v.tensor_tensor(pb[:, 0:512], pT[:, 0:512],
                                            rot4[:, ci * 512:(ci + 1) * 512],
                                            OP.mult)
                            v.tensor_tensor(pb[:, 512:1024], ctx.bank(7),
                                            rot4[:, si_ * 512:(si_ + 1) * 512],
                                            OP.mult)
                            v.tensor_tensor(tile, pb[:, 0:512], pb[:, 512:1024],
                                            OP.add)
                    ctx.cnt["lf_l2"] += 1
                    ctx.cnt["lf_l3"] += 2
                    ctx.cnt["lf_l4"] += 1
                    ctx.cnt["lf_l5"] += 1
            for h in (range(8) if attn_upto >= 3 else ()):
                hb = (h % 2) * 64
                htile = h // 2
                qh = qf[:, htile * 512:(htile + 1) * 512]
                s_sm = ctx.semaphore("lf_sm")
                s_sx = ctx.semaphore("lf_sx")
                sm0, sx0 = ctx.cnt["lf_sm"], ctx.cnt["lf_sx"]
                with nc.Block() as b:
                    @b.tensor
                    def _(t):
                        for tqs in range(4):
                            if tqs >= 2:
                                t.wait_ge(s_sx, sx0 + tqs - 1)
                            bk = ctx.bank(2 * (tqs % 2), 2)
                            for kk in range(2):
                                ksrc = kf2[:, prv * 2048 + htile * 512:
                                           prv * 2048 + (htile + 1) * 512] if kk == 0                                     else kf2[:, cur * 2048 + htile * 512:
                                             cur * 2048 + (htile + 1) * 512]
                                t.matmul(bk[:, kk * 512:(kk + 1) * 512],
                                         qh[hb:hb + 64, tqs * 128:(tqs + 1) * 128],
                                         ksrc[hb:hb + 64, :],
                                         start=True, stop=True).then_inc(s_sm, 1)
                    @b.vector
                    def _(v):
                        for tqs in range(4):
                            v.wait_ge(s_sm, sm0 + 2 * (tqs + 1))
                            mvar = (4 if c == 0 else 0) + tqs
                            v.tensor_tensor(sa[:], ctx.bank(2 * (tqs % 2), 2),
                                            mb[:, mvar * 1024:(mvar + 1) * 1024],
                                            OP.add)
                            v.reduce_max(mx4[:, tqs:tqs + 1], sa[:],
                                         axis=AX.X).then_inc(s_sx, 1)
                ctx.cnt["lf_sm"] += 8
                ctx.cnt["lf_sx"] += 4
                s_m8 = ctx.semaphore("lf_m8")
                m80 = ctx.cnt["lf_m8"]
                with nc.Block() as b:
                    @b.scalar
                    def _(s):
                        s.mul(mx4[:, 4:8], mx4[:, 0:4], 8.0).then_inc(s_m8, 1)
                    @b.tensor
                    def _(t):
                        t.wait_ge(s_m8, m80 + 1)
                        for tqs in range(4):
                            t.matmul(ctx.bank(4)[:, tqs * 128:(tqs + 1) * 128],
                                     mx4[:, 4 + tqs:5 + tqs].broadcast_to((128, 128)),
                                     idf2[:], start=True, stop=True).then_inc(s_m8, 1)
                ctx.cnt["lf_m8"] += 5
                with nc.Block() as b:
                    @b.scalar
                    def _(s):
                        s.copy(m8B[:], ctx.bank(4))
                if l == 0 and c == 0 and h == 0 and ctx.t.get("lf_dbg") is not None:
                    with nc.Block() as b:
                        @b.vector
                        def _(v):
                            v.tensor_copy(ctx.t["lf_dbg"][:, 0:512], m8B[:])
                            v.tensor_copy(ctx.t["lf_dbg"][:, 512:520], mx4[:])
                s_tm = ctx.semaphore("lf_tm")
                s_tb = ctx.semaphore("lf_tb")
                s_tx = ctx.semaphore("lf_tx")
                s_pv = ctx.semaphore("lf_pv")
                tm0, tb0, tx0, pv0 = (ctx.cnt["lf_tm"], ctx.cnt["lf_tb"],
                                      ctx.cnt["lf_tx"], ctx.cnt["lf_pv"])
                with nc.Block() as b:
                    @b.tensor
                    def _(t):
                        for step in range(10):
                            if step < 8:
                                tkt = step
                                par = tkt % 2
                                if tkt >= 2:
                                    t.wait_ge(s_tb, tb0 + tkt - 1)
                                ksrc = (kf2[:, prv * 2048 + htile * 512:]
                                        if tkt < 4
                                        else kf2[:, cur * 2048 + htile * 512:])
                                koff = (tkt % 4) * 128
                                t.matmul(ctx.bank(par),
                                         ksrc[hb:hb + 64, koff:koff + 128],
                                         qh[hb:hb + 64, :], start=True, stop=True,
                                         skip_group_check=True).then_inc(s_tm, 1)
                            if step >= 2:
                                tkt = step - 2
                                t.wait_ge(s_tx, tx0 + tkt + 1)
                                vsrc = (vt2[:, prv * 2048:] if tkt < 4
                                        else vt2[:, cur * 2048:])
                                voff = (tkt % 4) * 512 + h * 64
                                if hb == 0:
                                    t.matmul(ctx.bank(5)[0:64, :],
                                             vsrc[:, voff:voff + 64],
                                             pT[:, (tkt % 2) * 512:(tkt % 2) * 512 + 512],
                                             start=(tkt == 0), stop=(tkt == 7),
                                             skip_group_check=True)
                                else:
                                    t.matmul(ctx.bank(5)[64:128, :],
                                             vsrc[:, voff:voff + 64],
                                             pT[:, (tkt % 2) * 512:(tkt % 2) * 512 + 512],
                                             start=(tkt == 0), stop=(tkt == 7),
                                             tile_position=(0, 64),
                                             skip_group_check=True)
                                t.matmul(ctx.bank(6)[0:1, :],
                                         ctx.t["ones_col_bf"][:],
                                         pT[:, (tkt % 2) * 512:(tkt % 2) * 512 + 512],
                                         start=(tkt == 0), stop=(tkt == 7),
                                         skip_group_check=True).then_inc(s_pv, 1)
                    @b.vector
                    def _(v):
                        for tkt in range(8):
                            par = tkt % 2
                            v.wait_ge(s_tm, tm0 + tkt + 1)
                            if tkt >= 2:
                                v.wait_ge(s_tx, tx0 + tkt - 1)
                            mvar = 8 if (c == 0 and tkt < 4) else tkt
                            v.scalar_tensor_tensor(
                                pb[:, par * 512:par * 512 + 512],
                                ctx.bank(par), 8.0,
                                mbt[:, mvar * 512:(mvar + 1) * 512],
                                OP.mult, OP.subtract)
                            v.tensor_tensor(pb[:, par * 512:par * 512 + 512],
                                            pb[:, par * 512:par * 512 + 512],
                                            m8B[:], OP.subtract).then_inc(s_tb, 1)
                    @b.scalar
                    def _(s):
                        for tkt in range(8):
                            par = tkt % 2
                            s.wait_ge(s_tb, tb0 + tkt + 1)
                            if tkt >= 2:
                                s.wait_ge(s_pv, pv0 + tkt - 1)
                            s.activation(pT[:, par * 512:par * 512 + 512],
                                         pb[:, par * 512:par * 512 + 512],
                                         AF.Exp).then_inc(s_tx, 1)
                            if (tkt == 4 and l == 0 and c == 0 and h == 0
                                    and ctx.t.get("lf_dbg") is not None):
                                s.copy(ctx.t["lf_dbg"][:, 520:1032],
                                       pT[:, par * 512:par * 512 + 512])
                                s.copy(ctx.t["lf_dbg"][:, 1032:1544],
                                       pb[:, par * 512:par * 512 + 512])
                ctx.cnt["lf_tm"] += 8
                ctx.cnt["lf_tb"] += 8
                ctx.cnt["lf_tx"] += 8
                ctx.cnt["lf_pv"] += 8
                if l == 0 and c == 0 and h == 0 and ctx.t.get("lf_dbg") is not None:
                    with nc.Block() as b:
                        @b.vector
                        def _(v):
                            v.tensor_copy(ctx.t["lf_dbg"][:, 520:1032],
                                          pT[:, 0:512])
                            v.tensor_copy(ctx.t["lf_dbg"][:, 1032:1544],
                                          ctx.bank(5))
                            v.tensor_copy(ctx.t["lf_dbg"][0:1, 1544:2056],
                                          ctx.bank(6)[0:1, :])
                s_rn = ctx.semaphore("lf_rn")
                rn0 = ctx.cnt["lf_rn"]
                with nc.Block() as b:
                    @b.scalar
                    def _(s):
                        s.copy(rsr[:], ctx.bank(6)[0:1, :]).then_inc(s_rn, 1)
                with nc.Block() as b:
                    @b.vector
                    def _(v):
                        v.reciprocal(rsr[:], rsr[:])
                with nc.Block() as b:
                    @b.tensor
                    def _(t):
                        t.matmul(ctx.bank(7), ctx.t["ones_row"][:], rsr[:],
                                 start=True, stop=True)
                with nc.Block() as b:
                    @b.scalar
                    def _(s):
                        s.copy(rcpB[:], ctx.bank(7))
                with nc.Block() as b:
                    @b.vector
                    def _(v):
                        v.tensor_tensor(
                            o_all[hb:hb + 64, htile * 512:(htile + 1) * 512],
                            ctx.bank(5)[hb:hb + 64, :],
                            rcpB[hb:hb + 64, :], OP.mult)
                ctx.cnt["lf_rn"] += 1
            if attn_upto < 4:
                with nc.Block() as b:
                    @b.sync
                    def _(sy):
                        sy.dma_start(lout_h.rearrange("(fc p) t -> p fc t", p=128)
                                     [:, :, t0:t0 + TC], xlc_r).then_inc(s_st, 16)
                ctx.cnt["lf_st"] += 16
                continue
            s_wm = ctx.semaphore("lf_wm")
            s_we = ctx.semaphore("lf_we")
            wm0, we0 = ctx.cnt["lf_wm"], ctx.cnt["lf_we"]
            with nc.Block() as b:
                @b.tensor
                def _(t):
                    for dm in range(NF):
                        if dm >= 4:
                            t.wait_ge(s_we, we0 + dm - 3)
                        for kc in range(4):
                            ins = t.matmul(
                                ps[:, (dm % 4) * 512:(dm % 4) * 512 + 512],
                                wo[:, kc * 1024 + dm * 128:kc * 1024 + (dm + 1) * 128],
                                o_all[:, kc * 512:(kc + 1) * 512],
                                start=(kc == 0), stop=(kc == 3))
                        ins.then_inc(s_wm, 1)
                @b.vector
                def _(v):
                    for dm in range(NF):
                        v.wait_ge(s_wm, wm0 + dm + 1)
                        v.tensor_tensor(xlc[:, dm * TC:(dm + 1) * TC],
                                        ps[:, (dm % 4) * 512:(dm % 4) * 512 + 512],
                                        xlc[:, dm * TC:(dm + 1) * TC],
                                        OP.add).then_inc(s_we, 1)
                    for fc in range(NF):
                        v.tensor_copy(mcol2[:, 8 * cur + fc:8 * cur + fc + 1],
                                      xlc[:, fc * TC + 511:fc * TC + 512])
            ctx.cnt["lf_wm"] += NF
            ctx.cnt["lf_we"] += NF
            if not do_ffn:
                with nc.Block() as b:
                    @b.sync
                    def _(sy):
                        sy.dma_start(lout_h.rearrange("(fc p) t -> p fc t", p=128)
                                     [:, :, t0:t0 + TC], xlc_r).then_inc(s_st, 16)
                ctx.cnt["lf_st"] += 16
                continue
            with nc.Block() as b:
                @b.vector
                def _(v):
                    for fc in range(4):
                        v.tensor_copy(xs[:, fc * TC:(fc + 1) * TC],
                                      xlc[:, fc * TC:(fc + 1) * TC])
                    for fc in range(4, NF):
                        v.tensor_copy(xs[:, fc * TC + 1:(fc + 1) * TC],
                                      xlc[:, fc * TC:fc * TC + 511])
                        if c == 0:
                            v.memset(xs[:, fc * TC:fc * TC + 1], 0.0)
                        else:
                            v.tensor_copy(xs[:, fc * TC:fc * TC + 1],
                                          mcol2[:, 8 * prv + fc:8 * prv + fc + 1])
            xs_tiles = [xs[:, fc * TC:(fc + 1) * TC] for fc in range(NF)]
            emit_ln(ctx, xs_tiles, xs_tiles, TC, double=False,
                    gamma_cols=[lgc[:, l * 8 + fc:l * 8 + fc + 1] for fc in range(NF)],
                    beta_cols=[lbc[:, l * 8 + fc:l * 8 + fc + 1] for fc in range(NF)])
            s_f1l = ctx.semaphore("lf_f1l")
            s_f1m = ctx.semaphore("lf_f1m")
            s_f1s = ctx.semaphore("lf_f1s")
            s_f1v = ctx.semaphore("lf_f1v")
            f1l0, f1m0, f1s0, f1v0 = (ctx.cnt["lf_f1l"], ctx.cnt["lf_f1m"],
                                      ctx.cnt["lf_f1s"], ctx.cnt["lf_f1v"])
            with nc.Block() as b:
                @b.gpsimd
                def _(g):
                    for j in range(NA):
                        par = j % 2
                        if j >= 2:
                            g.wait_ge(s_f1m, f1m0 + j - 1)
                        g.dma_start(w1b.ap().rearrange("p (s kc i) -> p s kc i",
                                                       s=4, i=128)[:, 2 * par],
                                    w1a_h[l][:, j * 128:(j + 1) * 128]
                                    .rearrange("(kc p) i -> p kc i", p=128)
                                    ).then_inc(s_f1l, 16)
                        g.dma_start(w1b.ap().rearrange("p (s kc i) -> p s kc i",
                                                       s=4, i=128)[:, 2 * par + 1],
                                    w1g_h[l][:, j * 128:(j + 1) * 128]
                                    .rearrange("(kc p) i -> p kc i", p=128)
                                    ).then_inc(s_f1l, 16)
                @b.tensor
                def _(t):
                    for j in range(NA):
                        par = j % 2
                        t.wait_ge(s_f1l, f1l0 + 32 * (j + 1))
                        if j >= 2:
                            t.wait_ge(s_f1v, f1v0 + j - 1)
                        for kc in range(NF):
                            t.matmul(ctx.bank(2 * par),
                                     w1b[:, 2 * par * 1024 + kc * 128:
                                         2 * par * 1024 + (kc + 1) * 128],
                                     xs[:, kc * TC:(kc + 1) * TC],
                                     start=(kc == 0), stop=(kc == NF - 1))
                        for kc in range(NF):
                            ins = t.matmul(ctx.bank(2 * par + 1),
                                           w1b[:, (2 * par + 1) * 1024 + kc * 128:
                                               (2 * par + 1) * 1024 + (kc + 1) * 128],
                                           xs[:, kc * TC:(kc + 1) * TC],
                                           start=(kc == 0), stop=(kc == NF - 1))
                        ins.then_inc(s_f1m, 1)
                @b.scalar
                def _(s):
                    for j in range(NA):
                        par = j % 2
                        s.wait_ge(s_f1m, f1m0 + j + 1)
                        s.activation(gsl[:, par * 512:par * 512 + 512],
                                     ctx.bank(2 * par + 1), AF.Gelu,
                                     bias=b1gc[:, l * NA + j:l * NA + j + 1]
                                     ).then_inc(s_f1s, 1)
                @b.vector
                def _(v):
                    for j in range(NA):
                        par = j % 2
                        v.wait_ge(s_f1s, f1s0 + j + 1)
                        v.scalar_tensor_tensor(
                            gu_tc[:, j * 512:(j + 1) * 512],
                            ctx.bank(2 * par),
                            b1ac[:, l * NA + j:l * NA + j + 1],
                            gsl[:, par * 512:par * 512 + 512],
                            OP.add, OP.mult).then_inc(s_f1v, 1)
            ctx.cnt["lf_f1l"] += 32 * NA
            ctx.cnt["lf_f1m"] += NA
            ctx.cnt["lf_f1s"] += NA
            ctx.cnt["lf_f1v"] += NA
            s_f2l = ctx.semaphore("lf_f2l")
            s_f2m = ctx.semaphore("lf_f2m")
            f2l0, f2m0 = ctx.cnt["lf_f2l"], ctx.cnt["lf_f2m"]
            with nc.Block() as b:
                @b.gpsimd
                def _(g):
                    for k in range(NA):
                        par = k % 2
                        if k >= 2:
                            g.wait_ge(s_f2m, f2m0 + k - 1)
                        g.dma_start(w2b[:, par * 1024:(par + 1) * 1024],
                                    w2_h[l][k * 128:(k + 1) * 128, :]
                                    ).then_inc(s_f2l, 16)
                @b.tensor
                def _(t):
                    for k in range(NA):
                        par = k % 2
                        t.wait_ge(s_f2l, f2l0 + 16 * (k + 1))
                        for dm in range(NF):
                            ins = t.matmul(
                                ps[:, dm * 512:(dm + 1) * 512],
                                w2b[:, par * 1024 + dm * 128:par * 1024 + (dm + 1) * 128],
                                gu_tc[:, k * 512:(k + 1) * 512],
                                start=(k == 0), stop=(k == NA - 1))
                        ins.then_inc(s_f2m, 1)
                @b.vector
                def _(v):
                    v.wait_ge(s_f2m, f2m0 + NA)
                    for dm in range(NF):
                        v.scalar_tensor_tensor(
                            xlc[:, dm * TC:(dm + 1) * TC],
                            ps[:, dm * 512:(dm + 1) * 512],
                            b2c[:, l * 8 + dm:l * 8 + dm + 1],
                            xlc[:, dm * TC:(dm + 1) * TC],
                            OP.add, OP.add)
            ctx.cnt["lf_f2l"] += 16 * NA
            ctx.cnt["lf_f2m"] += NA
            with nc.Block() as b:
                @b.sync
                def _(sy):
                    sy.dma_start(lout_h.rearrange("(fc p) t -> p fc t", p=128)
                                 [:, :, t0:t0 + TC], xlc_r).then_inc(s_st, 16)
            ctx.cnt["lf_st"] += 16
    xl_fin = xl_h if nlayers != 1 else xl2_h
    for c in range(NTC):
        t0 = c * TC
        with nc.Block() as b:
            @b.sync
            def _(sy):
                sy.wait_ge(s_st, ctx.cnt["lf_st"])
                sy.dma_start(xlc_r, xin_r[:, :, t0:t0 + TC]).then_inc(s_ld, 16)
                sy.dma_start(gu_tc.ap().bitcast(F32)[:, 0:4096].rearrange(
                    "p (fc t) -> p fc t", t=TC),
                    xl_fin.rearrange("(fc p) t -> p fc t", p=128)[:, :, t0:t0 + TC]
                    ).then_inc(s_ld, 16)
        ctx.cnt["lf_ld"] += 32
        with nc.Block() as b:
            @b.vector
            def _(v):
                v.wait_ge(ctx.sem["lf_ld"], ctx.cnt["lf_ld"])
                xsf = gu_tc.ap().bitcast(F32)
                for fc in range(NF):
                    v.tensor_tensor(xlc[:, fc * TC:(fc + 1) * TC],
                                    xlc[:, fc * TC:(fc + 1) * TC],
                                    xsf[:, fc * TC:(fc + 1) * TC], OP.add)
        with nc.Block() as b:
            @b.sync
            def _(sy):
                sy.dma_start(xout_r[:, :, t0:t0 + TC], xlc_r).then_inc(s_st, 16)
        ctx.cnt["lf_st"] += 16


def prep_lformer_weights(inp):
    f = np.ascontiguousarray
    W = {}
    W["wqkv"] = f(inp["L0_ffn_wqkv"])
    W["wof"] = f(inp["L0_ffn_wo"])
    w1 = inp["L0_ffn_w1"]
    HID = 2730
    PADH = NA * 128
    w1a = np.zeros((2, 1024, PADH), np.float32); w1a[:, :, :HID] = w1[:, :, :HID]
    w1g = np.zeros((2, 1024, PADH), np.float32); w1g[:, :, :HID] = w1[:, :, HID:]
    w2 = inp["L0_ffn_w2"]
    w2p = np.zeros((2, PADH, 1024), np.float32); w2p[:, :HID, :] = w2
    W["w1a"], W["w1g"], W["w2p"] = f(w1a), f(w1g), f(w2p)
    b1 = inp["L0_ffn_b1"]
    b1a = np.zeros((2, PADH), np.float32); b1a[:, :HID] = b1[:, :HID]
    b1g = np.zeros((2, PADH), np.float32); b1g[:, :HID] = b1[:, HID:]
    W["b1a_r"] = f(b1a.reshape(2, NA, 128).transpose(2, 0, 1).reshape(128, 2 * NA))
    W["b1g_r"] = f(b1g.reshape(2, NA, 128).transpose(2, 0, 1).reshape(128, 2 * NA))
    W["b2_r"] = f(inp["L0_ffn_b2"].reshape(2, 8, 128).transpose(2, 0, 1)
                  .reshape(128, 16))
    W["qs_r"] = f(np.tile(inp["L0_ffn_q_scale"], (1, 2)).T)
    W["ks_r"] = f(np.tile(inp["L0_ffn_k_scale"], (1, 2)).T)
    W["lg_r"] = f(inp["L0_ffn_ln_g"].reshape(2, 8, 128).transpose(2, 0, 1)
                  .reshape(128, 16))
    W["lb_r"] = f(inp["L0_ffn_ln_b"].reshape(2, 8, 128).transpose(2, 0, 1)
                  .reshape(128, 16))
    blk = np.zeros((128, 128), np.float32)
    for p in range(128):
        for q in range(128):
            if p // 64 == q // 64:
                blk[p, q] = 1.0
    W["blk64"] = blk
    rott = np.zeros((128, 128), np.float32)
    for i in range(128):
        if i % 64 < 32:
            rott[i + 32, i] = -1.0
        else:
            rott[i - 32, i] = 1.0
    W["rott"] = rott
    W["ident"] = np.eye(128, dtype=np.float32)
    mb = np.zeros((128, 8 * 1024), np.float32)
    for v in range(8):
        c0 = v >= 4
        tqs = v % 4
        for p in range(128):
            qpos = tqs * 128 + p
            for col in range(1024):
                if col < 512:
                    vis = (not c0) and (col >= qpos)
                else:
                    vis = (col - 512) <= qpos
                if not vis:
                    mb[p, v * 1024 + col] = -1e9
    W["mb"] = mb
    mbt = np.zeros((128, 9 * 512), np.float32)
    for v in range(8):
        for p in range(128):
            for t in range(512):
                if v < 4:
                    vis = (v * 128 + p) >= t
                else:
                    vis = ((v - 4) * 128 + p) <= t
                if not vis:
                    mbt[p, v * 512 + t] = 8e9
    mbt[:, 8 * 512:] = 8e9
    W["mbt"] = mbt
    n = 4096
    half = np.arange(0, 64, 2, dtype=np.float64)
    inv_freq = 1.0 / 10000.0 ** (half / 64)
    t = np.arange(n, dtype=np.float64)
    freqs = t[:, None] * inv_freq[None, :]
    freqs = np.concatenate([freqs, freqs], axis=-1)
    base = (half + 0.4 * 64) / (1.4 * 64)
    power = (t - n // 2) / 256.0
    scale = base[None, :] ** power[:, None]
    scale = np.concatenate([scale, scale], axis=-1)
    cos, sin = np.cos(freqs), np.sin(freqs)
    rot4 = np.zeros((128, 4, 4096), np.float32)
    for p in range(128):
        ff = p % 64
        rot4[p, 0] = cos[:, ff] * scale[:, ff]
        rot4[p, 1] = sin[:, ff] * scale[:, ff]
        rot4[p, 2] = cos[:, ff] / scale[:, ff]
        rot4[p, 3] = sin[:, ff] / scale[:, ff]
    W["rot4"] = rot4
    return W


LF_SHAPES = dict(
    wqkv=(2, 1024, 1536), wof=(2, 512, 1024),
    w1a=(2, 1024, NA * 128), w1g=(2, 1024, NA * 128), w2p=(2, NA * 128, 1024),
    b1a_r=(128, 2 * NA), b1g_r=(128, 2 * NA), b2_r=(128, 16),
    qs_r=(128, 2), ks_r=(128, 2), lg_r=(128, 16), lb_r=(128, 16),
    blk64=(128, 128), rott=(128, 128), ident=(128, 128),
    mb=(128, 8 * 1024), mbt=(128, 9 * 512), rot4=(128, 4, 4096))


def build_lformer_program(nlayers=2, attn_upto=9, do_ffn=True, debug=False):
    nc = bass.Bass("TRN2", target_bir_lowering=False)
    W = {k: nc.dram_tensor("i_" + k, list(s), F32, kind="ExternalInput").ap()
         for k, s in LF_SHAPES.items()}
    xin = nc.dram_tensor("x", [1024, 4096], F32, kind="ExternalInput").ap()
    xout = nc.dram_tensor("xo", [1024, 4096], F32, kind="ExternalOutput").ap()
    st = contextlib.ExitStack()
    with st:
        ctx = Ctx(nc, st)
        alloc_common(ctx)
        if debug:
            dbg = ctx.sbuf("lf_dbg", [128, 2056], F32)
            d_dbg = nc.dram_tensor("d_dbg", [128, 2056], F32,
                                   kind="ExternalOutput").ap()
        emit_lformer(ctx, xin, xout, W, nlayers=nlayers, attn_upto=attn_upto, do_ffn=do_ffn)
        if debug:
            with nc.Block() as b:
                @b.sync
                def _(sy):
                    sy.wait_ge(ctx.sem["lf_st"], ctx.cnt["lf_st"])
                    sy.dma_start(d_dbg, dbg[:]).then_inc(ctx.sem["lf_st"], 16)
            ctx.cnt["lf_st"] += 16
        finish(ctx, ["lf_st"])
    return nc


import concourse.bass_utils as _bu


def kernel(**inputs):
    inp = {k: np.ascontiguousarray(np.asarray(v, dtype=np.float32))
           for k, v in inputs.items()}
    Wm = prep_mamba_weights(inp)
    Wl = prep_lformer_weights(inp)
    Wa = prep_l1attn_weights(inp)
    Wo = prep_moe_weights(inp)
    x = inp["L0_attn_x"]
    cur = [np.ascontiguousarray(x[c].T) for c in range(8)]
    progs = [(build_mamba_program(), Wm), (build_lformer_program(), Wl),
             (build_l1attn_program(), Wa), (build_moe_program(), Wo)]
    for nc, W in progs:
        in_maps = [{**{"i_" + k: W[k] for k in W}, "x": cur[c]} for c in range(8)]
        res = _bu.run_bass_kernel_spmd(nc, in_maps, core_ids=list(range(8)))
        cur = [np.ascontiguousarray(np.asarray(res.results[c]["xo"]))
               for c in range(8)]
    return np.stack([c.T for c in cur], axis=0)
